# version 20
# baseline (speedup 1.0000x reference)
import contextlib
import numpy as np
import ml_dtypes
import concourse.bass as bass
import concourse.mybir as mybir
from concourse.bass_utils import run_bass_kernel_spmd

F32 = mybir.dt.float32
F16 = mybir.dt.float16
I16 = mybir.dt.int16
BF16 = mybir.dt.bfloat16
AF = mybir.ActivationFunctionType
ALU = mybir.AluOpType
AX = mybir.AxisListType

D = 1024
NH = 16
HD = 64
MLP = 4096
FI = 2816
EPS = 1e-5
FT = D // 128
BF = np.float16


class Prog:
    STREAMS = ("tensor", "vector", "scalar", "sync", "gpsimd")
    NQSEM = 12

    def __init__(self, nc, ctx):
        self.nc = nc
        self.ctx = ctx
        self.sem = {s: ctx.enter_context(nc.semaphore(f"sem_{s}")) for s in
                    ("tensor", "vector", "scalar")}
        self.qsems = {}
        self.quse = {}
        self.qrr = {"sync": 0, "gpsimd": 0}
        for q in ("sync", "gpsimd"):
            for i in range(self.NQSEM):
                key = f"{q}{i}"
                self.sem[key] = ctx.enter_context(nc.semaphore(f"sem_{key}"))
                self.quse[key] = 0
        self.count = {"tensor": 0, "vector": 0, "scalar": 0}
        self.res = {}
        self.guar = {s: {} for s in self.STREAMS}
        self.clocks = {k: [] for k in self.sem}
        self.emits = {s: [] for s in self.STREAMS}
        self.selfwait = {}
        self.log = {s: [] for s in self.STREAMS}
        self.stats = {"units": 0, "waits": 0}

    def _semval(self, key, cnt):
        return cnt * 16 if key[:-1].rstrip("0123456789") in ("sync", "gpsimd") or \
            key[:4] in ("sync", "gpsi") else cnt

    def _queue_wait(self, stream, need):
        g = self.guar[stream]
        for key in sorted(need):
            cnt = need[key]
            if key == stream:
                if cnt <= self.selfwait.get(stream, 0):
                    pass
                sem = self.sem[key]
                self.emits[stream].append(lambda h, sem=sem, val=cnt: h.wait_ge(sem, val))
                self.stats["waits"] += 1
                g[key] = max(g.get(key, 0), cnt)
                continue
            if g.get(key, 0) >= cnt:
                continue
            sem = self.sem[key]
            val = cnt * 16 if (key.startswith("sync") or key.startswith("gpsimd")) else cnt
            self.emits[stream].append(lambda h, sem=sem, val=val: h.wait_ge(sem, val))
            self.stats["waits"] += 1
            g[key] = max(g.get(key, 0), cnt)

    def _deps(self, reads, writes):
        need = {}

        def add(m):
            if m and need.get(m[0], 0) < m[1]:
                need[m[0]] = m[1]

        for r in reads:
            e = self.res.get(r)
            if e:
                add(e["w"])
        for w in writes:
            e = self.res.get(w)
            if e:
                add(e["w"])
                for k, c in e["r"].items():
                    add((k, c))
        return need

    def _snapshot(self, stream, key, cnt):
        clk = dict(self.guar[stream])
        clk[key] = cnt
        self.clocks[key].append(clk)

    def _track(self, stream, key, cnt, reads, writes):
        for r in reads:
            e = self.res.setdefault(r, {"w": None, "r": {}})
            if e["r"].get(key, 0) < cnt:
                e["r"][key] = cnt
        mile = (key, cnt)
        for w in writes:
            self.res[w] = {"w": mile, "r": {}}
        self.stats["units"] += 1

    def op(self, stream, reads, writes, fn):
        assert stream in ("tensor", "vector", "scalar")
        self.log[stream].append((reads, writes))
        self._queue_wait(stream, self._deps(reads, writes))
        sem = self.sem[stream]
        self.emits[stream].append(lambda h, fn=fn, sem=sem: fn(h).then_inc(sem, 1))
        self.count[stream] += 1
        cnt = self.count[stream]
        self._snapshot(stream, stream, cnt)
        self._track(stream, stream, cnt, reads, writes)

    def dma(self, queue, out, in_, reads, writes, **kw):
        assert queue in ("sync", "gpsimd")
        need = self._deps(reads, writes)
        i = self.qrr[queue]
        self.qrr[queue] = (i + 1) % self.NQSEM
        key = f"{queue}{i}"
        prev = self.quse[key]
        if prev > 0 and need.get(key, 0) < prev:
            need[key] = prev
        self._queue_wait(queue, need)
        sem = self.sem[key]
        self.emits[queue].append(
            lambda h, out=out, in_=in_, kw=kw, sem=sem:
            h.dma_start(out=out, in_=in_, **kw).then_inc(sem, 16))
        self.quse[key] += 1
        cnt = self.quse[key]
        self._snapshot(queue, key, cnt)
        self._track(queue, key, cnt, reads, writes)

    def wait_all(self, stream):
        need = {s: c for s, c in self.count.items() if c > 0 and s != stream}
        for key, c in self.quse.items():
            if c > 0:
                need[key] = c
        self._queue_wait(stream, need)

    def union_keys(self, keys):
        readers = {}
        for k in keys:
            e = self.res.pop(k, None)
            if not e:
                continue
            ms = list(e["r"].items())
            if e["w"]:
                ms.append(e["w"])
            for s, c in ms:
                if readers.get(s, 0) < c:
                    readers[s] = c
        return readers

    def flush(self, block):
        for s, starter in (("sync", block.sync), ("tensor", block.tensor),
                           ("vector", block.vector), ("scalar", block.scalar),
                           ("gpsimd", block.gpsimd)):
            ems = self.emits[s]
            if not ems:
                continue

            def run(h, ems=ems):
                for e in ems:
                    e(h)
            starter(run)


class Buf:

    def __init__(self, p, name, ncols, dtype=F16):
        self.p = p
        self.t = p.ctx.enter_context(p.nc.sbuf_tensor(name, [128, ncols], dtype))
        self.name = name
        self.ncols = ncols
        self.ep = 0

    def role(self, bcols):
        old = [k for k in list(self.p.res) if k.startswith(f"{self.name}.")]
        self.ep += 1
        nblk = self.ncols // bcols
        union = self.p.union_keys(old)
        for b in range(nblk):
            self.p.res[f"{self.name}.{self.ep}:{b}"] = {"w": None, "r": dict(union)}
        return RoleView(self, bcols, self.ep, None)


class RoleView:
    def __init__(self, buf, bcols, ep, bar):
        self.t = buf.t
        self.name = buf.name
        self.ncols = buf.ncols
        self.bcols = bcols
        self.ep = ep
        self.bar = bar

    def blk(self, b, c0=0, c1=None, p0=0, p1=128):
        c1 = self.bcols if c1 is None else c1
        return self.t[p0:p1, b * self.bcols + c0: b * self.bcols + c1]

    def view3(self):
        nb = self.ncols // self.bcols
        return self.t[:, 0:nb * self.bcols].rearrange("p (b c) -> p b c", c=self.bcols)

    def k(self, b):
        return f"{self.name}.{self.ep}:{b}"

    def rk(self, blocks):
        return [self.k(b) for b in blocks]

    def wk(self, blocks):
        return [self.k(b) for b in blocks]


def build_core_program(T=2048, upto=4, shard_wq=False):
    TT = T // 128
    QC = max(1, T // 512)
    CW = min(512, T)
    nc = bass.Bass("TRN2", target_bir_lowering=False)
    dram = {}

    def din(name, shape, dtype=F32):
        dram[name] = nc.dram_tensor(name, list(shape), dtype, kind="ExternalInput").ap()

    din("x", (T, D))
    dram["y"] = nc.dram_tensor("y", [T, D], F32, kind="ExternalOutput").ap()
    din("identb", (128, 128), F16)
    din("identf", (128, 128), F32)
    din("ones_b", (1, 128), F16)
    din("headmask", (128, FT * NH))
    din("trimask", (128, 128), F16)

    if upto >= 1:
        for w in ("Wq", "Wk", "Wv", "Wfq", "Wfk"):
            din(w, (D, D))
        din("W1", (D, MLP))
        din("W2", (MLP, D))
        din("bfq_col", (128, FT))
        din("b1_col", (128, MLP // 128))
        for v in ("gq", "betaq", "gk", "betak"):
            din(v + "_col", (128, FT))
        for v in ("gv", "betav", "ga", "ba"):
            din(v + "_bc", (128, D))
        for v in ("bq_row", "bk_row", "bv_row", "bfk_row", "b2_row"):
            din(v, (1, D))
        dram["w2scr"] = nc.dram_tensor("w2scr", [MLP, D], F16, kind="Internal").ap()
    if upto >= 2:
        din("w_gate", (FI, D))
        din("w_up", (FI, D))
        din("w_down", (D, FI))
        if shard_wq == "local":
            dram["wgoA"] = nc.dram_tensor("wgoA", [44 * 128, D], F16, kind="Internal").ap()
            dram["wgoB"] = nc.dram_tensor("wgoB", [8 * 128, FI], F16, kind="Internal").ap()
    if upto >= 3:
        din("awq", (D, D))
        din("awk", (D, D))
        din("awv", (D, D))
        din("out_W", (D, D))
        din("abq_col", (128, NH))
        din("abk_col", (128, NH))
        din("abv_row", (1, D))
        din("outb_row", (1, D))
        dram["x2scr"] = nc.dram_tensor("x2scr", [T, D], F16, kind="Internal").ap()
    if upto >= 4:
        din("W_r", (D, D))
        din("W_k", (D, D))
        din("W_v", (D, D))
        din("mu_r_col", (128, FT))
        din("mu_k_col", (128, FT))

    with contextlib.ExitStack() as ctx:
        p = Prog(nc, ctx)
        sb = lambda name, shape, dtype: ctx.enter_context(nc.sbuf_tensor(name, shape, dtype))

        NCOL = max(8 * T, 16384)
        G1 = Buf(p, "G1", NCOL)
        G2 = Buf(p, "G2", NCOL)
        G3 = Buf(p, "G3", NCOL)
        G4 = Buf(p, "G4", NCOL)
        GV = Buf(p, "GV", max(NCOL, TT * NH * 65))
        WAbB = Buf(p, "WAbB", 8 * 1024)
        ROT = Buf(p, "ROT", 2 * 1024).role(1024)
        xstage = sb("xstage", [128, D], F32)

        consts = {}
        for name, dt_ in (("identb", F16), ("identf", F32), ("ones_b", F16),
                          ("headmask", F32), ("trimask", F16)):
            consts[name] = sb(name + "_s", list(dram[name].shape), dt_)
            p.dma("sync", consts[name][:, :], dram[name][:, :], reads=[], writes=[name])

        STAT = sb("STAT", [128, 64], F32)
        QSC = sb("QSC", [128, 3072], F16)
        QG = sb("QG", [128, 640], F32)
        BR = sb("BR", [1, 2112], F16)
        DUM = sb("DUM", [128, 512], F16)
        KVS = sb("KVS", [128, 8 * 65], F16)
        QPC = sb("QPC", [128, FT], F32)
        QPBD = sb("QPBD", [128, FT * NH], F16)
        SR = sb("SR", [16, 4], F32)
        STc = sb("STc", [128, TT * NH], F32)
        COLS = {}

        PS = [ctx.enter_context(nc.psum_tensor(f"ps{i}", [128, 512], F32)) for i in range(6)]
        PSB = [ctx.enter_context(nc.psum_tensor(f"psb{i}", [128, 1024], F16)) for i in range(2)]
        psk = [f"ps:{i}" for i in range(6)]
        psbk = [f"psb:{i}" for i in range(2)]

        def load_cols(name, shape=None, dtype=F32):
            shape = shape or list(dram[name].shape)
            t = sb(name + "_s", shape, dtype)
            COLS[name] = t
            p.dma("sync", t[:, :], dram[name][:, :], reads=[], writes=[name])
            return t

        def load_row(name, slot):
            c0 = 0 if slot == 0 else 1056
            p.dma("gpsimd", BR[0:1, c0:c0 + D], dram[name][0:1, :],
                  reads=[], writes=[f"brow:{slot}"])
            return BR[0:1, c0:c0 + D]

        BCT = [sb("bct0", [128, D], F16), sb("bct1", [128, D], F16)]

        def cast_bc(name, slot):
            p.dma("sync", xstage[:, 0:D], dram[name][:, :], reads=[], writes=["xst:0"])
            p.op("scalar", ["xst:0"], [f"bct:{slot}"],
                 lambda h, slot=slot: h.copy(BCT[slot][:, :], xstage[:, 0:D]))
            return BCT[slot]

        def load_weight(dst: Buf, hbm, nrowblk, cols=D, name=None):
            name = name or hbm
            for rb in range(nrowblk):
                p.dma("gpsimd", dst.blk(rb, 0, cols),
                      dram[hbm][rb * 128:(rb + 1) * 128, 0:cols],
                      reads=[], writes=dst.wk([rb]))

        def mm_chain(psbanks, lhs_list, rhs_list, reads, extra=None):
            nb = len(psbanks)

            def fn(h):
                last = None
                nchain = len(lhs_list)
                for j in range(nchain):
                    for i in range(nb):
                        last = h.matmul(psbanks[i], lhs_list[j], rhs_list[j][i],
                                        start=(j == 0), stop=(j == nchain - 1 and not extra))
                if extra:
                    for idx, (el, er) in enumerate(extra):
                        for i in range(nb):
                            last = h.matmul(psbanks[i], el, er[i],
                                            start=False,
                                            stop=(idx == len(extra) - 1))
                return last
            return fn

        def phase0():
            XT = G1.role(T)
            for tt in range(TT):
                sl = tt % 2
                p.dma("gpsimd", ROT.blk(sl),
                      dram["x"][tt * 128:(tt + 1) * 128, :], reads=[], writes=ROT.wk([sl]))
                transpose_tile_to_T(ROT, sl, XT, tt)
            return XT

        tpev = [0]

        def transpose_tile_to_T(src, sblk, dstT, tt, nft=FT, f0=0):
            done = 0
            bk = 0
            while done < nft:
                nb = min(4, nft - done)
                bk = 1 - bk

                def mk(h, sblk=sblk, done=done, nb=nb, bk=bk):
                    last = None
                    for j in range(nb):
                        ft = done + j
                        last = h.transpose(PSB[bk][:, j * 128:(j + 1) * 128],
                                           src.blk(sblk, ft * 128, (ft + 1) * 128),
                                           consts["identb"][:, :])
                    return last
                p.op("tensor", [src.k(sblk), "identb"], [psbk[bk]], mk)
                tpev[0] += 1
                eng = "scalar" if tpev[0] % 2 == 0 else "vector"

                def ev(h, done=done, nb=nb, bk=bk, tt=tt, f0=f0, eng=eng):
                    dst = dstT.view3()[:, f0 + done:f0 + done + nb, tt * 128:tt * 128 + 128]
                    srcv = PSB[bk][:, 0:nb * 128].rearrange("p (b c) -> p b c", c=128)
                    return h.copy(dst, srcv) if eng == "scalar" else h.tensor_copy(dst, srcv)
                p.op(eng, [psbk[bk]], dstT.wk(range(f0 + done, f0 + done + nb)), ev)
                done += nb

        def ln_stats_math(slot):
            b0 = slot * 8
            sk = lambda i: f"st{slot}:{i}"
            p.op("vector", [sk(0), sk(1)], [sk(4)],
                 lambda h, b0=b0: h.tensor_tensor(STAT[:, b0 + 4:b0 + 5], STAT[:, b0:b0 + 1],
                                                  STAT[:, b0 + 1:b0 + 2], op=ALU.add))
            p.op("vector", [sk(4)], [sk(4)],
                 lambda h, b0=b0: h.tensor_scalar(STAT[:, b0 + 4:b0 + 5], STAT[:, b0 + 4:b0 + 5],
                                                  1.0 / D, None, op0=ALU.mult))
            p.op("vector", [sk(2), sk(3)], [sk(5)],
                 lambda h, b0=b0: h.tensor_tensor(STAT[:, b0 + 5:b0 + 6], STAT[:, b0 + 2:b0 + 3],
                                                  STAT[:, b0 + 3:b0 + 4], op=ALU.add))
            p.op("vector", [sk(5)], [sk(5)],
                 lambda h, b0=b0: h.tensor_scalar(STAT[:, b0 + 5:b0 + 6], STAT[:, b0 + 5:b0 + 6],
                                                  1.0 / D, None, op0=ALU.mult))
            p.op("vector", [sk(4)], [sk(6)],
                 lambda h, b0=b0: h.tensor_tensor(STAT[:, b0 + 6:b0 + 7], STAT[:, b0 + 4:b0 + 5],
                                                  STAT[:, b0 + 4:b0 + 5], op=ALU.mult))
            p.op("vector", [sk(5), sk(6)], [sk(5)],
                 lambda h, b0=b0: h.tensor_tensor(STAT[:, b0 + 5:b0 + 6], STAT[:, b0 + 5:b0 + 6],
                                                  STAT[:, b0 + 6:b0 + 7], op=ALU.subtract))
            p.op("vector", [sk(5)], [sk(5)],
                 lambda h, b0=b0: h.tensor_scalar(STAT[:, b0 + 5:b0 + 6], STAT[:, b0 + 5:b0 + 6],
                                                  float(EPS), None, op0=ALU.add))
            p.op("scalar", [sk(5)], [sk(7)],
                 lambda h, b0=b0: h.activation(STAT[:, b0 + 7:b0 + 8], STAT[:, b0 + 5:b0 + 6],
                                               AF.Sqrt))
            p.op("vector", [sk(7)], [sk(7)],
                 lambda h, b0=b0: h.reciprocal(STAT[:, b0 + 7:b0 + 8], STAT[:, b0 + 7:b0 + 8]))
            return (STAT[:, b0 + 4:b0 + 5], STAT[:, b0 + 7:b0 + 8], sk(4), sk(7))

        def phase1(XT):
            load_cols("bfq_col")
            p.op("vector", ["bfq_col"], ["bfq_col"],
                 lambda h: h.tensor_scalar(COLS["bfq_col"][:, :], COLS["bfq_col"][:, :],
                                           2.0, None, op0=ALU.mult))
            load_cols("b1_col")
            for v in ("gq", "betaq", "gk", "betak"):
                load_cols(v + "_col")
            bct = {"gv": cast_bc("gv_bc", 0), "betav": cast_bc("betav_bc", 1)}

            def proj_plain(Wbuf, brow, brkey, lnf):
                for tt in range(TT):
                    b0, b1_ = PS[(tt % 3) * 2], PS[(tt % 3) * 2 + 1]
                    lhs = [XT.blk(ft, tt * 128, tt * 128 + 128) for ft in range(FT)]
                    rhs = [[Wbuf.blk(ft, 0, 512), Wbuf.blk(ft, 512, 1024)] for ft in range(FT)]
                    extra = [(consts["ones_b"][0:1, 0:128],
                              [brow[0:1, 0:512], brow[0:1, 512:1024]])]
                    p.op("tensor",
                         XT.rk(range(FT)) + [Wbuf.k(ft) for ft in range(FT)] + ["ones_b", brkey],
                         [psk[(tt % 3) * 2], psk[(tt % 3) * 2 + 1]],
                         mm_chain([b0[:, :], b1_[:, :]], lhs, rhs, None, extra))
                    lnf(tt, (tt % 3) * 2)

            def ln_evac(tt, bki, dst, dstblk, gaincols=None, gbc=None, strided65=False):
                sl = tt % 2
                slot = tt % 4
                b0 = slot * 8
                sk = lambda i_: f"st{slot}:{i_}"
                p.op("scalar", [psk[bki]], ROT.wk([sl]) + [sk(0)],
                     lambda h, b0=b0: h.activation(ROT.blk(sl, 0, 512), PS[bki][:, :], AF.Copy,
                                                   accum_out=STAT[:, b0:b0 + 1]))
                p.op("scalar", [psk[bki + 1]], ROT.wk([sl]) + [sk(1)],
                     lambda h, b0=b0: h.activation(ROT.blk(sl, 512, 1024), PS[bki + 1][:, :],
                                                   AF.Copy, accum_out=STAT[:, b0 + 1:b0 + 2]))
                p.op("scalar", [psk[bki]], [sk(2), "DUMk"],
                     lambda h, b0=b0: h.activation(DUM[:, :], PS[bki][:, :], AF.Square,
                                                   accum_out=STAT[:, b0 + 2:b0 + 3]))
                p.op("scalar", [psk[bki + 1]], [sk(3), "DUMk"],
                     lambda h, b0=b0: h.activation(DUM[:, :], PS[bki + 1][:, :], AF.Square,
                                                   accum_out=STAT[:, b0 + 3:b0 + 4]))
                mAP, rAP, mk_, rk_ = ln_stats_math(slot)
                if gbc is None:
                    p.op("vector", [ROT.k(sl), mk_, rk_], dst.wk([dstblk]),
                         lambda h, mAP=mAP, rAP=rAP: h.tensor_scalar(
                             dst.blk(dstblk), ROT.blk(sl), mAP, rAP,
                             op0=ALU.subtract, op1=ALU.mult))
                else:
                    gt, bt = gbc
                    p.op("vector", [ROT.k(sl), mk_, rk_], ROT.wk([sl]),
                         lambda h, mAP=mAP, rAP=rAP: h.tensor_scalar(
                             ROT.blk(sl), ROT.blk(sl), mAP, rAP,
                             op0=ALU.subtract, op1=ALU.mult))
                    p.op("vector", [ROT.k(sl), "bct:0"], ROT.wk([sl]),
                         lambda h: h.tensor_tensor(ROT.blk(sl), ROT.blk(sl), gt[:, :], op=ALU.mult))
                    if not strided65:
                        p.op("vector", [ROT.k(sl), "bct:1"], dst.wk([dstblk]),
                             lambda h: h.tensor_tensor(dst.blk(dstblk), ROT.blk(sl), bt[:, :], op=ALU.add))
                    else:
                        def ev(h):
                            dv = dst.blk(dstblk).rearrange("p (g c) -> p g c", c=65)[:, :, 0:64]
                            sv = ROT.blk(sl).rearrange("p (g c) -> p g c", c=64)
                            return h.tensor_tensor(dv, sv, bt[:, :].rearrange("p (g c) -> p g c", c=64), op=ALU.add)
                        p.op("vector", [ROT.k(sl), "bct:1"], dst.wk([dstblk]), ev)

            V65 = GV.role(NH * 65)
            p.op("vector", [], V65.wk(range(TT)),
                 lambda h: h.memset(V65.t[:, 0:TT * NH * 65], 1.0))
            Wv = WAbB.role(1024)
            load_weight(Wv, "Wv", FT)
            bv_r = load_row("bv_row", 0)
            proj_plain(Wv, bv_r, "brow:0",
                       lambda tt, bki: ln_evac(tt, bki, V65, tt,
                                               gbc=(bct["gv"], bct["betav"]), strided65=True))
            bg_drain(6)

            QT = G2.role(T)
            QPL = G4.role(D)
            Wq = WAbB.role(1024)
            load_weight(Wq, "Wq", FT)
            bq_r = load_row("bq_row", 0)

            def q_evac(tt, bki):
                ln_evac(tt, bki, QPL, tt, gaincols=None)
            proj_plain(Wq, bq_r, "brow:0", q_evac)
            bg_drain(6)
            for tt in range(TT):
                transpose_tile_to_T(QPL, tt, QT, tt)
            for ft in range(FT):
                p.op("vector", [QT.k(ft), "gq_col", "betaq_col"], QT.wk([ft]),
                     lambda h, ft=ft: h.tensor_scalar(QT.blk(ft), QT.blk(ft),
                                                      COLS["gq_col"][:, ft:ft + 1],
                                                      COLS["betaq_col"][:, ft:ft + 1],
                                                      op0=ALU.mult, op1=ALU.add))

            for ft in range(FT):
                p.op("vector", [QT.k(ft)], [f"qpc:{ft}"],
                     lambda h, ft=ft: h.reduce_sum(QPC[:, ft:ft + 1], QT.blk(ft), axis=AX.X))
                p.op("vector", [f"qpc:{ft}", "headmask"], [f"qpbd:{ft}"],
                     lambda h, ft=ft: h.tensor_scalar(
                         QPBD[:, ft * NH:(ft + 1) * NH],
                         consts["headmask"][:, ft * NH:(ft + 1) * NH],
                         QPC[:, ft:ft + 1], 1.0 / (T * 8.0),
                         op0=ALU.mult, op1=ALU.mult))

            PQT = G3.role(T)
            Wfq = WAbB.role(1024)
            load_weight(Wfq, "Wfq", FT)
            for dt in range(FT):
                for qc in range(QC):
                    bk = (dt * QC + qc) % 3
                    lhs = [Wfq.blk(ft, dt * 128, dt * 128 + 128) for ft in range(FT)]
                    rhs = [[QT.blk(ft, qc * CW, qc * CW + CW)] for ft in range(FT)]
                    p.op("tensor", [Wfq.k(ft) for ft in range(FT)] + QT.rk(range(FT)),
                         [psk[bk]],
                         mm_chain([PS[bk][:, 0:CW]], lhs, rhs, None))
                    p.op("scalar", [psk[bk], "bfq_col"], PQT.wk([dt]),
                         lambda h, dt=dt, qc=qc, bk=bk: h.activation(
                             PQT.blk(dt, qc * CW, qc * CW + CW), PS[bk][:, 0:CW], AF.Sigmoid,
                             bias=COLS["bfq_col"][:, dt:dt + 1], scale=2.0))

            KT = G2.role(T)
            QPL = G4.role(D)
            Wk = WAbB.role(1024)
            load_weight(Wk, "Wk", FT)
            bk_r = load_row("bk_row", 0)

            def k_evac(tt, bki):
                ln_evac(tt, bki, QPL, tt)
            proj_plain(Wk, bk_r, "brow:0", k_evac)
            bg_drain(6)
            for tt in range(TT):
                transpose_tile_to_T(QPL, tt, KT, tt)
            for ft in range(FT):
                p.op("vector", [KT.k(ft), "gk_col", "betak_col"], KT.wk([ft]),
                     lambda h, ft=ft: h.tensor_scalar(KT.blk(ft), KT.blk(ft),
                                                      COLS["gk_col"][:, ft:ft + 1],
                                                      COLS["betak_col"][:, ft:ft + 1],
                                                      op0=ALU.mult, op1=ALU.add))

            bg_drain(6)
            for qc in range(QC):
                bk = qc % 3
                lhs = [QPBD[:, ft * NH:(ft + 1) * NH] for ft in range(FT)]
                rhs = [[KT.blk(ft, qc * CW, qc * CW + CW)] for ft in range(FT)]
                p.op("tensor", [f"qpbd:{ft}" for ft in range(FT)] + KT.rk(range(FT)),
                     [psk[bk]],
                     mm_chain([PS[bk][0:16, 0:CW]], lhs, rhs, None))
                p.op("scalar", [psk[bk]], [f"G4.SS:{qc}"],
                     lambda h, qc=qc, bk=bk: h.activation(G4.t[0:16, qc * CW:qc * CW + CW],
                                                          PS[bk][0:16, 0:CW], AF.Exp))
            PHIK = G1.role(D)
            Wfk = WAbB.role(1024)
            load_weight(Wfk, "Wfk", FT)
            bfk_r = load_row("bfk_row", 1)
            for tt in range(TT):
                bki = (tt % 3) * 2
                lhs = [KT.blk(ft, tt * 128, tt * 128 + 128) for ft in range(FT)]
                rhs = [[Wfk.blk(ft, 0, 512), Wfk.blk(ft, 512, 1024)] for ft in range(FT)]
                extra = [(consts["ones_b"][0:1, 0:128],
                          [bfk_r[0:1, 0:512], bfk_r[0:1, 512:1024]])]
                p.op("tensor", KT.rk(range(FT)) + [Wfk.k(ft) for ft in range(FT)] + ["ones_b", "brow:1"],
                     [psk[bki], psk[bki + 1]],
                     mm_chain([PS[bki][:, :], PS[bki + 1][:, :]], lhs, rhs, None, extra))
                for half in range(2):
                    p.op("scalar", [psk[bki + half]], [f"phikr:{tt}:{half}"],
                         lambda h, tt=tt, bki=bki, half=half: h.activation(
                             PHIK.blk(tt, half * 512, half * 512 + 512),
                             PS[bki + half][:, :], AF.Sigmoid, scale=2.0))
            p.op("vector", [f"G4.SS:{qc}" for qc in range(QC)], ["srden"],
                 lambda h: h.reduce_sum(SR[0:16, 0:1], G4.t[0:16, 0:T], axis=AX.X))
            p.op("vector", ["srden"], ["srden"],
                 lambda h: h.tensor_scalar(SR[0:16, 0:1], SR[0:16, 0:1], 1.0, None, op0=ALU.add))
            p.op("vector", ["srden"], ["s0"],
                 lambda h: h.reciprocal(SR[0:16, 1:2], SR[0:16, 0:1]))
            p.op("vector", [f"G4.SS:{qc}" for qc in range(QC)] + ["s0"], ["G4.SSn"],
                 lambda h: h.tensor_scalar(G4.t[0:16, 0:T], G4.t[0:16, 0:T], SR[0:16, 1:2],
                                           None, op0=ALU.mult))
            for tt in range(TT):
                bk2 = tt % 2

                def mkt(h, tt=tt, bk2=bk2):
                    return h.transpose(PSB[bk2][0:128, 0:16],
                                       G4.t[0:16, tt * 128:(tt + 1) * 128],
                                       consts["identb"][0:16, 0:16])
                p.op("tensor", ["G4.SSn", "identf"], [psbk[bk2]], mkt)
                p.op("vector", [psbk[bk2]], [f"stc:{tt}"],
                     lambda h, tt=tt, bk2=bk2: h.tensor_copy(
                         STc[:, tt * NH:(tt + 1) * NH], PSB[bk2][0:128, 0:16]))

            for tt in range(TT):
                p.op("vector", [f"phikr:{tt}:0", f"phikr:{tt}:1", f"stc:{tt}"], PHIK.wk([tt]),
                     lambda h, tt=tt: h.tensor_tensor(
                         PHIK.blk(tt).rearrange("p (g c) -> p g c", c=64),
                         PHIK.blk(tt).rearrange("p (g c) -> p g c", c=64),
                         STc[:, tt * NH:(tt + 1) * NH].unsqueeze(2).broadcast_to([128, NH, 64]),
                         op=ALU.mult))

            p.op("scalar", ["brow:1"], ["brow:1"],
                 lambda h: h.activation(BR[0:1, 1056:1056 + D], BR[0:1, 1056:1056 + D],
                                        AF.Sigmoid, scale=2.0))
            p.op("tensor", ["s0", "identf"], [psk[5]],
                 lambda h: h.transpose(PS[5][0:1, 0:16], SR[0:16, 1:2],
                                       consts["identf"][0:16, 0:16]))
            p.op("vector", [psk[5]], ["s0r"],
                 lambda h: h.tensor_copy(STAT[0:1, 48:64], PS[5][0:1, 0:16]))
            p.op("vector", [], ["brow:0"],
                 lambda h: h.memset(BR[0:1, 0:NH * 65], 0.0))
            p.op("vector", ["s0r"], ["brow:0"],
                 lambda h: h.tensor_copy(
                     BR[0:1, 0:NH * 65].rearrange("p (g c) -> p g c", c=65)[:, :, 64:65],
                     STAT[0:1, 48:64].rearrange("p (g c) -> p g c", c=1)))

            bg_drain(6)
            for h_ in range(NH):
                blk, base = h_ // 2, (h_ % 2) * 64
                bk = h_ % 2

                def mkv(h, h_=h_, blk=blk, base=base, bk=bk):
                    last = None
                    for tt in range(TT):
                        last = h.matmul(PS[bk][base:base + 64, 0:65],
                                        PHIK.blk(tt, h_ * 64, h_ * 64 + 64),
                                        V65.blk(tt, h_ * 65, (h_ + 1) * 65),
                                        start=(tt == 0), stop=False)
                    last = h.matmul(PS[bk][base:base + 64, 0:65],
                                    BR[0:1, 1056 + h_ * 64: 1056 + (h_ + 1) * 64],
                                    BR[0:1, h_ * 65: (h_ + 1) * 65],
                                    start=False, stop=True)
                    return last
                p.op("tensor", PHIK.rk(range(TT)) + V65.rk(range(TT)) + ["brow:1", "brow:0"],
                     [psk[bk]], mkv)
                p.op("vector", [psk[bk]], [f"kvs:{h_}"],
                     lambda h, h_=h_, blk=blk, base=base, bk=bk: h.tensor_copy(
                         KVS[base:base + 64, blk * 65:(blk + 1) * 65],
                         PS[bk][base:base + 64, 0:65]))

            A = G2.role(D)
            for tt in range(TT):
                grp = tt % 2
                pbase = 3 * grp
                for h_ in range(NH):
                    bi = pbase + (h_ // 7 if h_ < 14 else 2)
                    col = (h_ % 7) * 65 if h_ < 14 else (h_ - 14) * 65
                    blk, base = h_ // 2, (h_ % 2) * 64

                    def mnum(h, tt=tt, h_=h_, bi=bi, col=col, blk=blk, base=base):
                        return h.matmul(PS[bi][:, col:col + 65],
                                        PQT.blk(blk, tt * 128, tt * 128 + 128, p0=base, p1=base + 64),
                                        KVS[base:base + 64, blk * 65:(blk + 1) * 65],
                                        start=True, stop=True)
                    p.op("tensor", [PQT.k(blk), f"kvs:{h_}"], [psk[bi]], mnum)
                def dget(h, tt=tt, pbase=pbase):
                    return h.tensor_copy(
                        STAT[:, 8:24],
                        PS[pbase][:, 0:455].rearrange("p (g c) -> p g c", c=65)[:, :, 64:65])
                p.op("vector", [psk[pbase]], ["dcol"],
                     lambda h, pbase=pbase: h.tensor_copy(
                         STAT[:, 8:15],
                         PS[pbase][:, 0:455].rearrange("p (g c) -> p g c", c=65)[:, :, 64:65].rearrange("p g c -> p (g c)")))
                p.op("vector", [psk[pbase + 1]], ["dcol2"],
                     lambda h, pbase=pbase: h.tensor_copy(
                         STAT[:, 15:22],
                         PS[pbase + 1][:, 0:455].rearrange("p (g c) -> p g c", c=65)[:, :, 64:65].rearrange("p g c -> p (g c)")))
                p.op("vector", [psk[pbase + 2]], ["dcol3"],
                     lambda h, pbase=pbase: h.tensor_copy(
                         STAT[:, 22:24],
                         PS[pbase + 2][:, 0:130].rearrange("p (g c) -> p g c", c=65)[:, :, 64:65].rearrange("p g c -> p (g c)")))
                p.op("vector", ["dcol", "dcol2", "dcol3"], ["dcol"],
                     lambda h: h.tensor_scalar(STAT[:, 8:24], STAT[:, 8:24], 2.5e-7, None, op0=ALU.add))
                p.op("vector", ["dcol"], ["dcol"],
                     lambda h: h.reciprocal(STAT[:, 8:24], STAT[:, 8:24]))
                for bnum, (h0, nh) in enumerate(((0, 7), (7, 7), (14, 2))):
                    bi = pbase + bnum
                    p.op("vector", [psk[bi], "dcol"], A.wk([tt]),
                         lambda h, tt=tt, bi=bi, h0=h0, nh=nh: h.tensor_tensor(
                             A.blk(tt, h0 * 64, (h0 + nh) * 64).rearrange(
                                 "p (g c) -> p g c", c=64),
                             PS[bi][:, 0:nh * 65].rearrange("p (g c) -> p g c", c=65)[:, :, 0:64],
                             STAT[:, 8 + h0:8 + h0 + nh].unsqueeze(2).broadcast_to([128, nh, 64]),
                             op=ALU.mult))

            bg_drain(6)
            cast_bc("ga_bc", 0)
            cast_bc("ba_bc", 1)
            for tt in range(TT):
                slot = tt % 4
                b0 = slot * 8
                sk = lambda i_: f"st{slot}:{i_}"
                for i_, (c0, func) in enumerate(((0, AF.Copy), (512, AF.Copy),
                                                 (0, AF.Square), (512, AF.Square))):
                    p.op("scalar", [A.k(tt)], [sk(i_), "DUMk"],
                         lambda h, tt=tt, c0=c0, func=func, i_=i_, b0=b0: h.activation(
                             DUM[:, :], A.blk(tt, c0, c0 + 512), func,
                             accum_out=STAT[:, b0 + i_:b0 + i_ + 1]))
                mAP, rAP, mk_, rk_ = ln_stats_math(slot)
                p.op("vector", [A.k(tt), mk_, rk_], A.wk([tt]),
                     lambda h, tt=tt, mAP=mAP, rAP=rAP: h.tensor_scalar(
                         A.blk(tt), A.blk(tt), mAP, rAP, op0=ALU.subtract, op1=ALU.mult))
                p.op("vector", [A.k(tt), "bct:0"], A.wk([tt]),
                     lambda h, tt=tt: h.tensor_tensor(A.blk(tt), A.blk(tt), BCT[0][:, :], op=ALU.mult))
                p.op("vector", [A.k(tt), "bct:1"], A.wk([tt]),
                     lambda h, tt=tt: h.tensor_tensor(A.blk(tt), A.blk(tt), BCT[1][:, :], op=ALU.add))

            bg_drain(6)
            AT = G1.role(T)
            for tt in range(TT):
                transpose_tile_to_T(A, tt, AT, tt)

            for tt in range(TT):
                sl = tt % 2
                p.dma("gpsimd", ROT.blk(sl),
                      dram["x"][tt * 128:(tt + 1) * 128, :], reads=[], writes=ROT.wk([sl]))
                p.op("vector", [ROT.k(sl), A.k(tt)], A.wk([tt]),
                     lambda h, tt=tt, sl=sl: h.scalar_tensor_tensor(
                         A.blk(tt), ROT.blk(sl), 2.0, A.blk(tt),
                         op0=ALU.mult, op1=ALU.add))
            R = A

            W1a = G4.role(MLP)
            load_weight(W1a, "W1", 4, cols=MLP)
            W1b = GV.role(MLP)
            for rb in range(4):
                p.dma("gpsimd", W1b.blk(rb, 0, MLP),
                      dram["W1"][(rb + 4) * 128:(rb + 5) * 128, 0:MLP],
                      reads=[], writes=W1b.wk([rb]))
            W2s = WAbB.role(1024)
            for rb in range(MLP // 128):
                p.dma("gpsimd", W2s.blk(rb % 8),
                      dram["W2"][rb * 128:(rb + 1) * 128, :],
                      reads=[], writes=W2s.wk([rb % 8]))
                p.dma("sync", dram["w2scr"][rb * 128:(rb + 1) * 128, :], W2s.blk(rb % 8),
                      reads=[W2s.k(rb % 8)], writes=[f"w2scr:{rb}"])

            GCH = G3.role(512)
            W2r = WAbB.role(1024)
            b2_r = load_row("b2_row", 0)
            TC = max(1, T // 512)
            NK2 = MLP // 128
            wq_bct[0] = True
            for tc in range(TC):
                bg_drain(2)
                for k2 in range(NK2):
                    bk = k2 % 2
                    lhs = [W1a.blk(ft, k2 * 128, (k2 + 1) * 128) for ft in range(4)] + \
                          [W1b.blk(ft - 4, k2 * 128, (k2 + 1) * 128) for ft in range(4, 8)]
                    rhs = [[AT.blk(ft, tc * CW, tc * CW + CW)] for ft in range(FT)]
                    p.op("tensor",
                         [W1a.k(f) for f in range(4)] + [W1b.k(f) for f in range(4)] + AT.rk(range(FT)),
                         [psk[bk]], mm_chain([PS[bk][:, 0:CW]], lhs, rhs, None))
                    p.op("scalar", [psk[bk], "b1_col"], GCH.wk([k2]),
                         lambda h, k2=k2, bk=bk: h.activation(
                             GCH.blk(k2, 0, CW), PS[bk][:, 0:CW], AF.Gelu,
                             bias=COLS["b1_col"][:, k2:k2 + 1]))
                ntt = CW // 128
                for jj in range(ntt // 2):
                    for g in range(NK2 // 8):
                        for kk in range(8):
                            k2 = g * 8 + kk
                            p.dma("sync", W2r.blk(kk), dram["w2scr"][k2 * 128:(k2 + 1) * 128, :],
                                  reads=[f"w2scr:{k2}"], writes=W2r.wk([kk]))
                        for jl in range(2):
                            j = jj * 2 + jl
                            bb = 2 + jl * 2

                            def mdown(h, j=j, g=g, bb=bb, last_g=(g == NK2 // 8 - 1)):
                                last = None
                                for kk in range(8):
                                    k2 = g * 8 + kk
                                    for half in range(2):
                                        last = h.matmul(
                                            PS[bb + half][:, :],
                                            GCH.blk(k2, j * 128, (j + 1) * 128),
                                            W2r.blk(kk, half * 512, half * 512 + 512),
                                            start=(g == 0 and kk == 0), stop=False)
                                if last_g:
                                    for half in range(2):
                                        last = h.matmul(PS[bb + half][:, :],
                                                        consts["ones_b"][0:1, 0:128],
                                                        b2_r[0:1, half * 512:half * 512 + 512],
                                                        start=False, stop=True)
                                return last
                            p.op("tensor",
                                 GCH.rk(range(g * 8, g * 8 + 8)) + [W2r.k(kk) for kk in range(8)] +
                                 ["ones_b", "brow:0"],
                                 [psk[bb], psk[bb + 1]], mdown)
                    for jl in range(2):
                        j = jj * 2 + jl
                        tt = tc * ntt + j
                        bb = 2 + jl * 2
                        sl = tt % 2
                        for half in range(2):
                            p.op("vector", [psk[bb + half], R.k(tt)], ROT.wk([sl]),
                                 lambda h, tt=tt, half=half, sl=sl, bb=bb: h.tensor_tensor(
                                     ROT.blk(sl, half * 512, half * 512 + 512),
                                     PS[bb + half][:, :],
                                     R.blk(tt, half * 512, half * 512 + 512), op=ALU.add))
                        ln_plain_tile(ROT, sl, R, tt)
                    bg_drain(2)
            bg_drain()
            wq_bct[0] = False

            return R

        def ln_plain_tile(src, sblk, dst, dblk, out_f32_stage=None):
            slot = dblk % 4
            b0 = slot * 8
            sk = lambda i_: f"st{slot}:{i_}"
            for i_, (c0, func) in enumerate(((0, AF.Copy), (512, AF.Copy),
                                             (0, AF.Square), (512, AF.Square))):
                p.op("scalar", [src.k(sblk)], [sk(i_), "DUMk"],
                     lambda h, c0=c0, func=func, i_=i_, b0=b0: h.activation(
                         DUM[:, :], src.blk(sblk, c0, c0 + 512), func,
                         accum_out=STAT[:, b0 + i_:b0 + i_ + 1]))
            mAP, rAP, mk_, rk_ = ln_stats_math(slot)
            p.op("vector", [src.k(sblk), mk_, rk_], dst.wk([dblk]),
                 lambda h, mAP=mAP, rAP=rAP: h.tensor_scalar(
                     dst.blk(dblk), src.blk(sblk), mAP, rAP,
                     op0=ALU.subtract, op1=ALU.mult))

        def quant_qd(srcap, srck, dstap, dstk, W):
            NG = W // 16
            MIDS = (0.25, 0.75, 1.25, 1.75, 2.5, 3.5, 5.0)
            sv = srcap.rearrange("p (g c) -> p g c", c=16)
            A = QSC[:, 0:W]
            Av = A.rearrange("p (g c) -> p g c", c=16)
            Q = QSC[:, 1024:1024 + W]
            Qv = Q.rearrange("p (g c) -> p g c", c=16)
            C = QSC[:, 2048:2048 + W]
            Cv = C.rearrange("p (g c) -> p g c", c=16)
            SG = dstap.bitcast(F16)
            g6b = QG[:, 64:64 + NG].unsqueeze(2).broadcast_to([128, NG, 16])
            WTS = (None, None, None, None, 2.0, 2.0, 4.0)

            r6b = QG[:, 128:128 + NG].unsqueeze(2).broadcast_to([128, NG, 16])
            p.op("vector", [srck], ["qg"],
                 lambda h: h.tensor_reduce(QG[:, 0:NG], sv, axis=AX.X, op=ALU.max,
                                           apply_absolute_value=True))
            p.op("vector", ["qg"], ["qg"],
                 lambda h: h.tensor_scalar(QG[:, 0:NG], QG[:, 0:NG], 6e-12, None, op0=ALU.max))
            p.op("vector", ["qg"], ["qg3"],
                 lambda h: h.tensor_scalar(QG[:, 64:64 + NG], QG[:, 0:NG], 1.0 / 6.0, None,
                                           op0=ALU.mult))
            def emit_sg():
                p.op("vector", [srck], [dstk],
                     lambda h: h.tensor_scalar(SG, srcap, 0.0, 0.5, op0=ALU.is_gt,
                                               op1=ALU.subtract))
            if srcap.dtype == F32:
                p.op("vector", ["qg3"], ["qt0"],
                     lambda h: h.reciprocal(QG[:, 128:128 + NG], QG[:, 64:64 + NG]))
                emit_sg()
                p.op("scalar", [srck], [srck],
                     lambda h: h.activation(srcap, srcap, AF.Abs))
                p.op("vector", [srck, "qt0"], [srck],
                     lambda h: h.tensor_tensor(sv, sv, r6b, op=ALU.mult))
                cmp_reads = [srck]

                def cmp(dstf, dstv, j):
                    if WTS[j] is None:
                        return lambda h: h.tensor_scalar(dstf, srcap, MIDS[j], None,
                                                         op0=ALU.is_gt)
                    return lambda h: h.tensor_scalar(dstf, srcap, MIDS[j], WTS[j],
                                                     op0=ALU.is_gt, op1=ALU.mult)
                cmp_extra = lambda j: []
            else:
                p.op("vector", ["qg3"], ["qt0"],
                     lambda h: h.reciprocal(QG[:, 128:128 + NG], QG[:, 64:64 + NG]))
                p.op("scalar", [srck], ["qA"],
                     lambda h: h.activation(A, srcap, AF.Abs))
                emit_sg()
                p.op("vector", ["qA", "qt0"], ["qA"],
                     lambda h: h.tensor_tensor(Av, Av, r6b, op=ALU.mult))
                cmp_reads = ["qA"]

                def cmp(dstf, dstv, j):
                    if WTS[j] is None:
                        return lambda h: h.tensor_scalar(dstf, A, MIDS[j], None,
                                                         op0=ALU.is_gt)
                    return lambda h: h.tensor_scalar(dstf, A, MIDS[j], WTS[j],
                                                     op0=ALU.is_gt, op1=ALU.mult)
                cmp_extra = lambda j: []
            p.op("vector", cmp_reads + cmp_extra(0), ["qQ"], cmp(Q, Qv, 0))
            for j in (1, 2, 3, 4, 5, 6):
                p.op("vector", cmp_reads + cmp_extra(j), ["qC"], cmp(C, Cv, j))
                p.op("vector", ["qQ", "qC"], ["qQ"],
                     lambda h: h.tensor_tensor(Q, Q, C, op=ALU.add))
            p.op("vector", ["qQ", dstk], ["qQ"],
                 lambda h: h.tensor_tensor(Q, Q, SG, op=ALU.mult))
            p.op("vector", ["qQ", "qg3", srck], [dstk],
                 lambda h: h.tensor_tensor(dstap.rearrange("p (g c) -> p g c", c=16), Qv, g6b,
                                           op=ALU.mult))

        def phase2(X1, XQTslot, OUTslot, W1slot, WVslot):
            QTR = [(0, 768), (768, 768), (1536, 768), (2304, 512)]
            XQT = XQTslot.role(T)
            for tt in range(TT):
                sl = tt % 2
                quant_qd(X1.blk(tt), X1.k(tt), ROT.blk(sl), ROT.k(sl), D)
                transpose_tile_to_T(ROT, sl, XQT, tt)
            OUT = OUTslot.role(D)
            for qi, (r0, RQ) in enumerate(QTR):
                NB = RQ // 128
                nch = (RQ + 511) // 512
                WGT = W1slot.role(RQ)
                for half, wsrc, toff in ((0, "w_gate", 0), (1, "w_up", FI // 128)):
                    WST = WAbB.role(1024)
                    for wb in range(NB):
                        t_ = toff + r0 // 128 + wb
                        if shard_wq == "local":
                            p.dma("sync", WST.blk(wb), dram["wgoA"][t_ * 128:(t_ + 1) * 128, :],
                                  reads=[f"wgo:False:{t_ * 128}:0"], writes=WST.wk([wb]))
                        else:
                            p.dma("sync", xstage[:, 0:D],
                                  dram[wsrc][r0 + wb * 128:r0 + (wb + 1) * 128, :],
                                  reads=["xst:0"], writes=["xst:0"])
                            quant_qd(xstage[:, 0:D], "xst:0", WST.blk(wb), WST.k(wb), D)
                    for wb in range(NB):
                        transpose_tile_to_T(WST, wb, WGT, wb, f0=half * FT)
                WDS = WAbB.role(RQ)
                for rb in range(FT):
                    if shard_wq == "local":
                        rdk = [f"wgo:True:{rb * 128}:{cc}" for cc in (0, 1024, 2048)]
                        p.dma("sync", WDS.blk(rb), dram["wgoB"][rb * 128:(rb + 1) * 128, r0:r0 + RQ],
                              reads=rdk, writes=WDS.wk([rb]))
                    else:
                        p.dma("sync", xstage[:, 0:RQ],
                              dram["w_down"][rb * 128:(rb + 1) * 128, r0:r0 + RQ],
                              reads=["xst:0"], writes=["xst:0"])
                        quant_qd(xstage[:, 0:RQ], "xst:0", WDS.blk(rb), WDS.k(rb), RQ)
                GT = WVslot.role(RQ)

                def gate_mm(tt):
                    bks = [(tt % 2) * 2, (tt % 2) * 2 + 1][:nch]
                    lhs = [XQT.blk(ft, tt * 128, tt * 128 + 128) for ft in range(FT)]
                    rhs = [[WGT.blk(ft, c * 512, min(RQ, (c + 1) * 512)) for c in range(nch)]
                           for ft in range(FT)]
                    p.op("tensor", XQT.rk(range(FT)) + WGT.rk(range(FT)),
                         [psk[b] for b in bks],
                         mm_chain([PS[bks[c]][:, 0:min(512, RQ - c * 512)] for c in range(nch)],
                                  lhs, rhs, None))
                    for c in range(nch):
                        w0, w1_ = c * 512, min(RQ, (c + 1) * 512)
                        p.op("scalar", [psk[bks[c]]], GT.wk([tt]),
                             lambda h, tt=tt, w0=w0, w1_=w1_, b=bks[c], GT=GT: h.activation(
                                 GT.blk(tt, w0, w1_), PS[b][:, 0:w1_ - w0], AF.Silu))

                def up_mm(tt):
                    bks = [4, 5][:nch]
                    lhs = [XQT.blk(ft, tt * 128, tt * 128 + 128) for ft in range(FT)]
                    rhs = [[WGT.blk(FT + ft, c * 512, min(RQ, (c + 1) * 512)) for c in range(nch)]
                           for ft in range(FT)]
                    p.op("tensor", XQT.rk(range(FT)) + WGT.rk(range(FT, 2 * FT)),
                         [psk[b] for b in bks],
                         mm_chain([PS[bks[c]][:, 0:min(512, RQ - c * 512)] for c in range(nch)],
                                  lhs, rhs, None))
                    for c in range(nch):
                        w0, w1_ = c * 512, min(RQ, (c + 1) * 512)
                        p.op("vector", [psk[bks[c]], GT.k(tt)], GT.wk([tt]),
                             lambda h, tt=tt, w0=w0, w1_=w1_, b=bks[c], GT=GT: h.tensor_tensor(
                                 GT.blk(tt, w0, w1_), PS[b][:, 0:w1_ - w0],
                                 GT.blk(tt, w0, w1_), op=ALU.mult))
                    quant_qd(GT.blk(tt, 0, RQ), GT.k(tt), GT.blk(tt, 0, RQ), GT.k(tt), RQ)

                gate_mm(0)
                for tt in range(1, TT):
                    gate_mm(tt)
                    up_mm(tt - 1)
                up_mm(TT - 1)
                HQT = W1slot.role(T)
                for tt in range(TT):
                    transpose_tile_to_T(GT, tt, HQT, tt, nft=NB)
                WDT = WVslot.role(1024)
                for rb in range(FT):
                    transpose_tile_to_T(WDS, rb, WDT, rb, nft=NB)
                for tt in range(TT):
                    bks = [4, 5]
                    lhs = [HQT.blk(k2, tt * 128, tt * 128 + 128) for k2 in range(NB)]
                    rhs = [[WDT.blk(k2, 0, 512), WDT.blk(k2, 512, 1024)] for k2 in range(NB)]
                    p.op("tensor", HQT.rk(range(NB)) + WDT.rk(range(NB)),
                         [psk[4], psk[5]],
                         mm_chain([PS[4][:, :], PS[5][:, :]], lhs, rhs, None))
                    for c in range(2):
                        if qi == 0:
                            p.op("scalar", [psk[4 + c]], OUT.wk([tt]),
                                 lambda h, tt=tt, c=c: h.activation(
                                     OUT.blk(tt, c * 512, (c + 1) * 512), PS[4 + c][:, :],
                                     AF.Copy))
                        else:
                            p.op("vector", [psk[4 + c], OUT.k(tt)], OUT.wk([tt]),
                                 lambda h, tt=tt, c=c: h.tensor_tensor(
                                     OUT.blk(tt, c * 512, (c + 1) * 512), PS[4 + c][:, :],
                                     OUT.blk(tt, c * 512, (c + 1) * 512), op=ALU.add))
            for tt in range(TT):
                sl = tt % 2
                p.op("vector", [X1.k(tt), OUT.k(tt)], ROT.wk([sl]),
                     lambda h, tt=tt, sl=sl: h.tensor_tensor(
                         ROT.blk(sl), X1.blk(tt), OUT.blk(tt), op=ALU.add))
                ln_plain_tile(ROT, sl, X1, tt)
            return X1

        def phase3(X2):
            for tt in range(TT):
                p.dma("sync", dram["x2scr"][tt * 128:(tt + 1) * 128, :], X2.blk(tt),
                      reads=[X2.k(tt)], writes=[f"x2scr:{tt}"])
            X2T = G1.role(T)
            for tt in range(TT):
                transpose_tile_to_T(X2, tt, X2T, tt)
            load_cols("abq_col")
            load_cols("abk_col")

            AWQK = G3.role(1024)
            for ft in range(FT):
                p.dma("gpsimd", AWQK.blk(ft), dram["awq"][ft * 128:(ft + 1) * 128, :],
                      reads=[], writes=AWQK.wk([ft]))
                p.dma("gpsimd", AWQK.blk(8 + ft), dram["awk"][ft * 128:(ft + 1) * 128, :],
                      reads=[], writes=AWQK.wk([8 + ft]))
            AWV = WAbB.role(1024)
            for ft in range(FT):
                p.dma("gpsimd", AWV.blk(ft), dram["awv"][ft * 128:(ft + 1) * 128, :],
                      reads=[], writes=AWV.wk([ft]))
            abv_r = load_row("abv_row", 0)

            QT3 = G2.role(T)
            KT3 = G4.role(T)
            for dst, colsk, boff, qscale in ((QT3, "abq_col", 0, 0.125), (KT3, "abk_col", 8, None)):
                for t in range(NH // 2):
                    for qc in range(QC):
                        bk = (t * QC + qc) % 3
                        lhs = [AWQK.blk(boff + ft, t * 128, (t + 1) * 128) for ft in range(FT)]
                        rhs = [[X2T.blk(ft, qc * CW, qc * CW + CW)] for ft in range(FT)]
                        p.op("tensor", AWQK.rk(range(boff, boff + FT)) + X2T.rk(range(FT)),
                             [psk[bk]],
                             mm_chain([PS[bk][:, 0:CW]], lhs, rhs, None))
                        p.op("vector", [psk[bk], colsk], dst.wk([t]),
                             lambda h, dst=dst, t=t, qc=qc, bk=bk, colsk=colsk, qs=qscale:
                             h.tensor_scalar(dst.blk(t, qc * CW, qc * CW + CW),
                                             PS[bk][:, 0:CW],
                                             COLS[colsk][:, t:t + 1], qs,
                                             op0=ALU.add,
                                             **({"op1": ALU.mult} if qs else {})))
            V65 = GV.role(NH * 65)
            p.op("vector", [], V65.wk(range(TT)),
                 lambda h: h.memset(V65.t[:, 0:TT * NH * 65], 1.0))
            for tt in range(TT):
                bks = [(tt % 2) * 2, (tt % 2) * 2 + 1]
                lhs = [X2T.blk(ft, tt * 128, tt * 128 + 128) for ft in range(FT)]
                rhs = [[AWV.blk(ft, 0, 512), AWV.blk(ft, 512, 1024)] for ft in range(FT)]
                extra = [(consts["ones_b"][0:1, 0:128],
                          [abv_r[0:1, 0:512], abv_r[0:1, 512:1024]])]
                p.op("tensor", X2T.rk(range(FT)) + AWV.rk(range(FT)) + ["ones_b", "brow:0"],
                     [psk[bks[0]], psk[bks[1]]],
                     mm_chain([PS[bks[0]][:, :], PS[bks[1]][:, :]], lhs, rhs, None, extra))
                for c in range(2):
                    def vev(h, tt=tt, c=c, b=bks[c]):
                        dv = V65.blk(tt).rearrange("p (g c) -> p g c", c=65)[:, 8 * c:8 * c + 8, 0:64]
                        return h.tensor_copy(dv, PS[b][:, :].rearrange("p (g c) -> p g c", c=64))
                    p.op("vector", [psk[bks[c]]], V65.wk([tt]), vev)

            CTb = G1.role(T)
            RD = BR
            for h_ in range(NH):
                t, base = h_ // 2, (h_ % 2) * 64
                for qc in range(QC):
                    ktmax = min(TT, (qc + 1) * CW // 128)
                    ctb = 4 if (qc % 2 == 0) else 5

                    def s_mm(kt):
                        bk = kt % 3
                        p.op("tensor", [KT3.k(t), QT3.k(t)], [psk[bk]],
                             lambda h, t=t, base=base, kt=kt, qc=qc, bk=bk:
                             h.matmul(PS[bk][:, 0:CW],
                                      KT3.blk(t, kt * 128, kt * 128 + 128, p0=base, p1=base + 64),
                                      QT3.blk(t, qc * CW, qc * CW + CW, p0=base, p1=base + 64),
                                      start=True, stop=True))
                        b0 = kt * 128 - qc * CW
                        sl = kt % 2
                        p.op("scalar", [psk[bk]], ROT.wk([sl]),
                             lambda h, sl=sl, bk=bk: h.activation(
                                 ROT.blk(sl, 0, CW), PS[bk][:, 0:CW], AF.Exp))
                        if b0 > 0:
                            p.op("vector", [ROT.k(sl)], ROT.wk([sl]),
                                 lambda h, sl=sl, b0=b0: h.memset(ROT.blk(sl, 0, b0), 0.0))
                        if b0 >= 0:
                            p.op("vector", [ROT.k(sl), "trimask"], ROT.wk([sl]),
                                 lambda h, sl=sl, b0=b0: h.tensor_tensor(
                                     ROT.blk(sl, b0, b0 + 128), ROT.blk(sl, b0, b0 + 128),
                                     consts["trimask"][:, :], op=ALU.mult))

                    def pv_mm(kt):
                        sl = kt % 2
                        p.op("tensor", [ROT.k(sl), V65.k(kt)], [psk[ctb]],
                             lambda h, h_=h_, kt=kt, sl=sl, ctb=ctb, ktmax=ktmax: h.matmul(
                                 PS[ctb][0:65, 0:CW],
                                 V65.blk(kt, h_ * 65, (h_ + 1) * 65),
                                 ROT.blk(sl, 0, CW),
                                 start=(kt == 0), stop=(kt == ktmax - 1)))
                    s_mm(0)
                    for kt in range(1, ktmax):
                        s_mm(kt)
                        pv_mm(kt - 1)
                    pv_mm(ktmax - 1)
                    p.op("vector", [psk[ctb], "xst:0"], ["xst:0"],
                         lambda h, ctb=ctb: h.reciprocal(xstage[0:1, 0:CW], PS[ctb][64:65, 0:CW]))
                    p.op("vector", ["xst:0"], ["brow:0"],
                         lambda h: h.tensor_copy(BR[0:1, 0:CW], xstage[0:1, 0:CW]))
                    p.op("tensor", ["brow:0", "ones_b"], [psk[3]],
                         lambda h: h.matmul(PS[3][0:64, 0:CW], consts["ones_b"][0:1, 0:64],
                                            BR[0:1, 0:CW], start=True, stop=True))
                    p.op("scalar", [psk[ctb]], ROT.wk([0]),
                         lambda h, ctb=ctb: h.activation(ROT.blk(0, 0, CW, p0=0, p1=64),
                                                         PS[ctb][0:64, 0:CW], AF.Copy))
                    p.op("vector", [ROT.k(0), psk[3]], CTb.wk([t]),
                         lambda h, t=t, base=base, qc=qc: h.tensor_tensor(
                             CTb.blk(t, qc * CW, qc * CW + CW, p0=base, p1=base + 64),
                             ROT.blk(0, 0, CW, p0=0, p1=64), PS[3][0:64, 0:CW],
                             op=ALU.mult))

            OWT = WAbB.role(1024)
            load_weight(OWT, "out_W", FT)
            outb_r = load_row("outb_row", 0)
            X3 = G4.role(D)
            for tt in range(TT):
                bks = [(tt % 2) * 2, (tt % 2) * 2 + 1]
                lhs = [CTb.blk(ftb, tt * 128, tt * 128 + 128) for ftb in range(FT)]
                rhs = [[OWT.blk(ftb, 0, 512), OWT.blk(ftb, 512, 1024)] for ftb in range(FT)]
                extra = [(consts["ones_b"][0:1, 0:128],
                          [outb_r[0:1, 0:512], outb_r[0:1, 512:1024]])]
                p.op("tensor", CTb.rk(range(FT)) + OWT.rk(range(FT)) + ["ones_b", "brow:0"],
                     [psk[bks[0]], psk[bks[1]]],
                     mm_chain([PS[bks[0]][:, :], PS[bks[1]][:, :]], lhs, rhs, None, extra))
                sl = tt % 2
                p.dma("sync", ROT.blk(sl), dram["x2scr"][tt * 128:(tt + 1) * 128, :],
                      reads=[f"x2scr:{tt}"], writes=ROT.wk([sl]))
                for c in range(2):
                    p.op("vector", [psk[bks[c]], ROT.k(sl)], ROT.wk([sl]),
                         lambda h, sl=sl, c=c, b=bks[c]: h.tensor_tensor(
                             ROT.blk(sl, c * 512, (c + 1) * 512), PS[b][:, :],
                             ROT.blk(sl, c * 512, (c + 1) * 512), op=ALU.add))
                ln_plain_tile(ROT, sl, X3, tt)
            return X3

        def phase4(X3):
            X3T = G1.role(T)
            for tt in range(TT):
                transpose_tile_to_T(X3, tt, X3T, tt)
            load_cols("mu_r_col")
            load_cols("mu_k_col")
            OMU = sb("OMU", [128, 2 * FT], F32)
            for i, v in enumerate(("mu_r_col", "mu_k_col")):
                p.op("vector", [v], [f"omu:{i}"],
                     lambda h, i=i, v=v: h.tensor_scalar(
                         OMU[:, i * FT:(i + 1) * FT], COLS[v][:, :], -1.0, 1.0,
                         op0=ALU.mult, op1=ALU.add))
            WTT = G3.role(1024)
            WVT4 = GV.role(1024)
            for wi, (wname, blks) in enumerate((("W_r", range(0, 8)), ("W_k", range(8, 16)))):
                WS = WAbB.role(1024)
                load_weight(WS, wname, FT)
                for rb in range(FT):
                    transpose_tile_to_T(WS, rb, WTT, rb) if False else None
                for rb in range(FT):
                    done = 0
                    bkk = 0
                    while done < FT:
                        nb = min(4, FT - done)
                        bkk = 1 - bkk

                        def mk(h, rb=rb, done=done, nb=nb, bkk=bkk, WS=WS):
                            last = None
                            for j in range(nb):
                                ft = done + j
                                last = h.transpose(PSB[bkk][:, j * 128:(j + 1) * 128],
                                                   WS.blk(rb, ft * 128, (ft + 1) * 128),
                                                   consts["identb"][:, :])
                            return last
                        p.op("tensor", [WS.k(rb), "identb"], [psbk[bkk]], mk)
                        blist = list(blks)[done:done + nb]

                        def ev(h, rb=rb, nb=nb, bkk=bkk, blist=tuple(blist)):
                            return h.tensor_copy(
                                WTT.view3()[:, blist[0]:blist[0] + nb, rb * 128:rb * 128 + 128],
                                PSB[bkk][:, 0:nb * 128].rearrange("p (b c) -> p b c", c=128))
                        p.op("vector", [psbk[bkk]], WTT.wk(blist), ev)
                        done += nb
            WS = WAbB.role(1024)
            load_weight(WS, "W_v", FT)
            for rb in range(FT):
                transpose_tile_to_T(WS, rb, WVT4, rb)

            CH = G4.role(512) if False else None
            X4 = X3
            CHB = GV
            CH = G2.role(CW)

            def mixes(qc):
                for mi, boff in ((0, 0), (1, 8)):
                    mcol = ("mu_r_col", "mu_k_col")[mi]
                    for ft in range(FT):
                        p.op("vector", [X3T.k(ft), mcol], CH.wk([boff + ft]),
                             lambda h, ft=ft, qc=qc, boff=boff, mcol=mcol: h.tensor_scalar(
                                 CH.blk(boff + ft), X3T.blk(ft, qc * CW, qc * CW + CW),
                                 COLS[mcol][:, ft:ft + 1], None, op0=ALU.mult))
                        if qc == 0:
                            p.op("vector", [X3T.k(ft), f"omu:{mi}", CH.k(boff + ft)],
                                 CH.wk([boff + ft]),
                                 lambda h, ft=ft, boff=boff, mi=mi: h.scalar_tensor_tensor(
                                     CH.blk(boff + ft, 1, CW),
                                     X3T.blk(ft, 0, CW - 1),
                                     OMU[:, mi * FT + ft:mi * FT + ft + 1],
                                     CH.blk(boff + ft, 1, CW),
                                     op0=ALU.mult, op1=ALU.add))
                        else:
                            p.op("vector", [X3T.k(ft), f"omu:{mi}", CH.k(boff + ft)],
                                 CH.wk([boff + ft]),
                                 lambda h, ft=ft, qc=qc, boff=boff, mi=mi: h.scalar_tensor_tensor(
                                     CH.blk(boff + ft, 0, CW),
                                     X3T.blk(ft, qc * CW - 1, (qc + 1) * CW - 1),
                                     OMU[:, mi * FT + ft:mi * FT + ft + 1],
                                     CH.blk(boff + ft, 0, CW),
                                     op0=ALU.mult, op1=ALU.add))

            def rk_kv(qc):
                for oi, (wblks, srcoff, dstoff, actf) in enumerate((
                        (range(0, 8), 0, 16, AF.Sigmoid), (range(8, 16), 8, 24, AF.Relu))):
                    for dt in range(FT):
                        bk = dt % 3
                        lhs = [WTT.blk(list(wblks)[ft], dt * 128, dt * 128 + 128)
                               for ft in range(FT)]
                        rhs = [[CH.blk(srcoff + ft)] for ft in range(FT)]
                        p.op("tensor", WTT.rk(wblks) + CH.rk(range(srcoff, srcoff + FT)),
                             [psk[bk]], mm_chain([PS[bk][:, 0:CW]], lhs, rhs, None))
                        p.op("scalar", [psk[bk]], CH.wk([dstoff + dt]),
                             lambda h, dt=dt, bk=bk, dstoff=dstoff, actf=actf: h.activation(
                                 CH.blk(dstoff + dt), PS[bk][:, 0:CW], actf))
                for dt in range(FT):
                    p.op("vector", [CH.k(24 + dt)], CH.wk([24 + dt]),
                         lambda h, dt=dt: h.tensor_tensor(CH.blk(24 + dt), CH.blk(24 + dt),
                                                          CH.blk(24 + dt), op=ALU.mult))
                for dt in range(FT):
                    bk = dt % 3
                    lhs = [WVT4.blk(ft, dt * 128, dt * 128 + 128) for ft in range(FT)]
                    rhs = [[CH.blk(24 + ft)] for ft in range(FT)]
                    p.op("tensor", WVT4.rk(range(FT)) + CH.rk(range(24, 32)),
                         [psk[bk]], mm_chain([PS[bk][:, 0:CW]], lhs, rhs, None))
                    p.op("vector", [psk[bk], CH.k(16 + dt)], CH.wk([16 + dt]),
                         lambda h, dt=dt, bk=bk: h.tensor_tensor(
                             CH.blk(16 + dt), PS[bk][:, 0:CW], CH.blk(16 + dt), op=ALU.mult))

            def tail(qc):
                ntt = CW // 128
                for j in range(ntt):
                    tt = qc * ntt + j
                    sl = 1
                    done = 0
                    bkk = 0
                    while done < FT:
                        nb = min(4, FT - done)
                        bkk = 1 - bkk

                        def mk(h, j=j, done=done, nb=nb, bkk=bkk, CH=CH):
                            last = None
                            for jj in range(nb):
                                last = h.transpose(PSB[bkk][:, jj * 128:(jj + 1) * 128],
                                                   CH.blk(16 + done + jj, j * 128, (j + 1) * 128),
                                                   consts["identb"][:, :])
                            return last
                        p.op("tensor", CH.rk(range(16 + done, 16 + done + nb)) + ["identb"],
                             [psbk[bkk]], mk)

                        def ev(h, done=done, nb=nb, bkk=bkk, sl=sl):
                            return h.tensor_copy(
                                ROT.blk(sl, done * 128, done * 128 + nb * 128).rearrange(
                                    "p (b c) -> p b c", c=128),
                                PSB[bkk][:, 0:nb * 128].rearrange("p (b c) -> p b c", c=128))
                        p.op("vector", [psbk[bkk]], ROT.wk([sl]), ev)
                        done += nb
                    p.op("vector", [ROT.k(sl), X4.k(tt)], ROT.wk([sl]),
                         lambda h, tt=tt, sl=sl: h.tensor_tensor(
                             ROT.blk(sl), ROT.blk(sl), X4.blk(tt), op=ALU.add))
                    ln_plain_tile(ROT, sl, X4, tt)

            mixes(0)
            for qc in range(QC):
                rk_kv(qc)
                if qc + 1 < QC:
                    mixes(qc + 1)
                tail(qc)
            return X4

        wq_thunks = []
        wq_bct = [False]

        def wq_local_thunks():
            NWT = FI // 128

            def tile_thunk(srcdram, r0w, dstdram, d0w, cc, cw):
                def th():
                    sl = (d0w + cc) // 512 % 2
                    if wq_bct[0]:
                        dstap, dstk = BCT[sl][:, 0:cw], f"bct:{sl}"
                    else:
                        dstap, dstk = ROT.blk(sl, 0, cw), ROT.k(sl)
                    p.dma("gpsimd", xstage[:, 0:cw], srcdram[r0w:r0w + 128, cc:cc + cw],
                          reads=["xst:0"], writes=["xst:0"])
                    quant_qd(xstage[:, 0:cw], "xst:0", dstap, dstk, cw)
                    p.dma("gpsimd", dstdram[d0w:d0w + 128, cc:cc + cw], dstap,
                          reads=[dstk], writes=[f"wgo:{dstdram is dram['wgoB']}:{d0w}:{cc}"])
                return th
            NWTD = FI // 128
            for t in range(NWTD):
                wq_thunks.append(tile_thunk(dram["w_gate"], t * 128, dram["wgoA"], t * 128, 0, D))
            for t in range(NWTD):
                wq_thunks.append(tile_thunk(dram["w_up"], t * 128, dram["wgoA"],
                                            (NWTD + t) * 128, 0, D))
            for t in range(FT):
                for cc, cw in ((0, 1024), (1024, 1024), (2048, 768)):
                    wq_thunks.append(tile_thunk(dram["w_down"], t * 128, dram["wgoB"],
                                                t * 128, cc, cw))

        def bg_drain(n=10**9):
            while wq_thunks and n > 0:
                wq_thunks.pop(0)()
                n -= 1

        if upto >= 2 and shard_wq == "local":
            wq_local_thunks()
        XT = phase0()
        stream = phase1(XT)
        bg_drain()
        if upto >= 2:
            stream = phase2(stream, G1, G3, G4, GV)
        if upto >= 3:
            stream = phase3(stream)
        if upto >= 4:
            stream = phase4(stream)

        for tt in range(TT):
            p.op("scalar", [stream.k(tt), "xst:0"], ["xst:0"],
                 lambda h, tt=tt: h.copy(xstage[:, 0:D], stream.blk(tt)))
            p.dma("sync", dram["y"][tt * 128:(tt + 1) * 128, :],
                  xstage[:, 0:D],
                  reads=["xst:0"], writes=["xst:0"])
        p.wait_all("sync")

        block = ctx.enter_context(nc.Block())
        p.flush(block)

    return nc, p


def prep_consts(T):
    TT = T // 128
    headmask = np.zeros((128, FT * NH), np.float32)
    for ft in range(FT):
        for pp in range(128):
            h = (ft * 128 + pp) // HD
            headmask[pp, ft * NH + h] = 1.0
    tri = np.tril(np.ones((128, 128), np.float32)).astype(BF)
    return {
        "identb": np.eye(128).astype(BF),
        "identf": np.eye(128, dtype=np.float32),
        "ones_b": np.ones((1, 128), BF),
        "headmask": headmask,
        "trimask": np.triu(np.ones((128, 128), np.float32)).astype(BF),
    }


def prep_phase1_weights(ins):
    d = {}
    for w in ("Wq", "Wk", "Wv", "Wfq", "Wfk"):
        d[w] = np.ascontiguousarray(ins["L0_attn_" + w])
    d["W1"] = np.ascontiguousarray(ins["L0_attn_W1"])
    d["W2"] = np.ascontiguousarray(ins["L0_attn_W2"])
    col = lambda v: np.ascontiguousarray(v.reshape(-1, 128).T)
    d["bfq_col"] = col(ins["L0_attn_bfq"])
    d["b1_col"] = col(ins["L0_attn_b1"])
    d["gq_col"] = col(ins["L0_attn_gq"])
    d["betaq_col"] = col(ins["L0_attn_betaq"])
    d["gk_col"] = col(ins["L0_attn_gk"])
    d["betak_col"] = col(ins["L0_attn_betak"])
    for v, src in (("gv", "gv"), ("betav", "betav"), ("ga", "ga"), ("ba", "ba")):
        d[v + "_bc"] = np.ascontiguousarray(
            np.broadcast_to(ins["L0_attn_" + src][None, :], (128, D)))
    for v, src in (("bq_row", "bq"), ("bk_row", "bk"), ("bv_row", "bv"),
                   ("bfk_row", "bfk"), ("b2_row", "b2")):
        d[v] = np.ascontiguousarray(ins["L0_attn_" + src][None, :])
    return d


def prep_phase2_weights(ins):
    return {
        "w_gate": np.ascontiguousarray(ins["L0_ffn_w_gate"]),
        "w_up": np.ascontiguousarray(ins["L0_ffn_w_up"]),
        "w_down": np.ascontiguousarray(ins["L0_ffn_w_down"]),
    }


def prep_phase3_weights(ins):
    ab = np.asarray(ins["L1_attn_attn_b"]).reshape(NH, 3 * HD)
    abq = np.zeros((128, NH), np.float32)
    abk = np.zeros((128, NH), np.float32)
    for t in range(NH // 2):
        abq[0:64, t] = ab[2 * t, 0:64];   abq[64:128, t] = ab[2 * t + 1, 0:64]
        abk[0:64, t] = ab[2 * t, 64:128]; abk[64:128, t] = ab[2 * t + 1, 64:128]
    abv = ab[:, 128:192].reshape(1, D)
    W = np.asarray(ins["L1_attn_attn_W"]).reshape(D, NH, 3 * HD)
    awq = W[:, :, 0:64].reshape(D, D)
    awk = W[:, :, 64:128].reshape(D, D)
    awv = W[:, :, 128:192].reshape(D, D)
    return {
        "awq": np.ascontiguousarray(awq),
        "awk": np.ascontiguousarray(awk),
        "awv": np.ascontiguousarray(awv),
        "out_W": np.ascontiguousarray(ins["L1_attn_out_W"]),
        "abq_col": abq, "abk_col": abk,
        "abv_row": np.ascontiguousarray(abv),
        "outb_row": np.ascontiguousarray(ins["L1_attn_out_b"][None, :]),
    }


def prep_phase4_weights(ins):
    col = lambda v: np.ascontiguousarray(v.reshape(-1, 128).T)
    return {
        "W_r": np.ascontiguousarray(ins["L1_ffn_W_r"]),
        "W_k": np.ascontiguousarray(ins["L1_ffn_W_k"]),
        "W_v": np.ascontiguousarray(ins["L1_ffn_W_v"]),
        "mu_r_col": col(ins["L1_ffn_mu_r"]),
        "mu_k_col": col(ins["L1_ffn_mu_k"]),
    }


def kernel(**inputs):
    T = 2048
    UPTO = 4
    nc, _ = build_core_program(T=T, upto=UPTO, shard_wq="local")
    consts = prep_consts(T)
    wts = prep_phase1_weights(inputs)
    if UPTO >= 2:
        wts.update(prep_phase2_weights(inputs))
    if UPTO >= 3:
        wts.update(prep_phase3_weights(inputs))
    if UPTO >= 4:
        wts.update(prep_phase4_weights(inputs))
    x = np.asarray(inputs["L0_attn_x"])
    in_maps = []
    for c in range(8):
        m = dict(consts)
        m.update(wts)
        m["x"] = np.ascontiguousarray(x[c])
        in_maps.append(m)
    res = run_bass_kernel_spmd(nc, in_maps, core_ids=list(range(8)))
    return np.stack([res.results[c]["y"] for c in range(8)], axis=0)


# version 21 (final)
# speedup vs baseline: 1.0213x; 1.0213x over previous
import contextlib
import numpy as np
import ml_dtypes
import concourse.bass as bass
import concourse.mybir as mybir
from concourse.bass_utils import run_bass_kernel_spmd

F32 = mybir.dt.float32
F16 = mybir.dt.float16
I16 = mybir.dt.int16
BF16 = mybir.dt.bfloat16
AF = mybir.ActivationFunctionType
ALU = mybir.AluOpType
AX = mybir.AxisListType

D = 1024
NH = 16
HD = 64
MLP = 4096
FI = 2816
EPS = 1e-5
FT = D // 128
BF = np.float16


class Prog:
    STREAMS = ("tensor", "vector", "scalar", "sync", "gpsimd")
    NQSEM = 12

    def __init__(self, nc, ctx):
        self.nc = nc
        self.ctx = ctx
        self.sem = {s: ctx.enter_context(nc.semaphore(f"sem_{s}")) for s in
                    ("tensor", "vector", "scalar")}
        self.qsems = {}
        self.quse = {}
        self.qrr = {"sync": 0, "gpsimd": 0}
        for q in ("sync", "gpsimd"):
            for i in range(self.NQSEM):
                key = f"{q}{i}"
                self.sem[key] = ctx.enter_context(nc.semaphore(f"sem_{key}"))
                self.quse[key] = 0
        self.count = {"tensor": 0, "vector": 0, "scalar": 0}
        self.res = {}
        self.guar = {s: {} for s in self.STREAMS}
        self.clocks = {k: [] for k in self.sem}
        self.emits = {s: [] for s in self.STREAMS}
        self.selfwait = {}
        self.log = {s: [] for s in self.STREAMS}
        self.stats = {"units": 0, "waits": 0}

    def _semval(self, key, cnt):
        return cnt * 16 if key[:-1].rstrip("0123456789") in ("sync", "gpsimd") or \
            key[:4] in ("sync", "gpsi") else cnt

    def _queue_wait(self, stream, need):
        g = self.guar[stream]
        for key in sorted(need):
            cnt = need[key]
            if key == stream:
                if cnt <= self.selfwait.get(stream, 0):
                    pass
                sem = self.sem[key]
                self.emits[stream].append(lambda h, sem=sem, val=cnt: h.wait_ge(sem, val))
                self.stats["waits"] += 1
                g[key] = max(g.get(key, 0), cnt)
                continue
            if g.get(key, 0) >= cnt:
                continue
            sem = self.sem[key]
            val = cnt * 16 if (key.startswith("sync") or key.startswith("gpsimd")) else cnt
            self.emits[stream].append(lambda h, sem=sem, val=val: h.wait_ge(sem, val))
            self.stats["waits"] += 1
            g[key] = max(g.get(key, 0), cnt)

    def _deps(self, reads, writes):
        need = {}

        def add(m):
            if m and need.get(m[0], 0) < m[1]:
                need[m[0]] = m[1]

        for r in reads:
            e = self.res.get(r)
            if e:
                add(e["w"])
        for w in writes:
            e = self.res.get(w)
            if e:
                add(e["w"])
                for k, c in e["r"].items():
                    add((k, c))
        return need

    def _snapshot(self, stream, key, cnt):
        clk = dict(self.guar[stream])
        clk[key] = cnt
        self.clocks[key].append(clk)

    def _track(self, stream, key, cnt, reads, writes):
        for r in reads:
            e = self.res.setdefault(r, {"w": None, "r": {}})
            if e["r"].get(key, 0) < cnt:
                e["r"][key] = cnt
        mile = (key, cnt)
        for w in writes:
            self.res[w] = {"w": mile, "r": {}}
        self.stats["units"] += 1

    def op(self, stream, reads, writes, fn):
        assert stream in ("tensor", "vector", "scalar")
        self.log[stream].append((reads, writes))
        self._queue_wait(stream, self._deps(reads, writes))
        sem = self.sem[stream]
        self.emits[stream].append(lambda h, fn=fn, sem=sem: fn(h).then_inc(sem, 1))
        self.count[stream] += 1
        cnt = self.count[stream]
        self._snapshot(stream, stream, cnt)
        self._track(stream, stream, cnt, reads, writes)

    def dma(self, queue, out, in_, reads, writes, **kw):
        assert queue in ("sync", "gpsimd")
        need = self._deps(reads, writes)
        i = self.qrr[queue]
        self.qrr[queue] = (i + 1) % self.NQSEM
        key = f"{queue}{i}"
        prev = self.quse[key]
        if prev > 0 and need.get(key, 0) < prev:
            need[key] = prev
        self._queue_wait(queue, need)
        sem = self.sem[key]
        self.emits[queue].append(
            lambda h, out=out, in_=in_, kw=kw, sem=sem:
            h.dma_start(out=out, in_=in_, **kw).then_inc(sem, 16))
        self.quse[key] += 1
        cnt = self.quse[key]
        self._snapshot(queue, key, cnt)
        self._track(queue, key, cnt, reads, writes)

    def wait_all(self, stream):
        need = {s: c for s, c in self.count.items() if c > 0 and s != stream}
        for key, c in self.quse.items():
            if c > 0:
                need[key] = c
        self._queue_wait(stream, need)

    def union_keys(self, keys):
        readers = {}
        for k in keys:
            e = self.res.pop(k, None)
            if not e:
                continue
            ms = list(e["r"].items())
            if e["w"]:
                ms.append(e["w"])
            for s, c in ms:
                if readers.get(s, 0) < c:
                    readers[s] = c
        return readers

    def flush(self, block):
        for s, starter in (("sync", block.sync), ("tensor", block.tensor),
                           ("vector", block.vector), ("scalar", block.scalar),
                           ("gpsimd", block.gpsimd)):
            ems = self.emits[s]
            if not ems:
                continue

            def run(h, ems=ems):
                for e in ems:
                    e(h)
            starter(run)


class Buf:

    def __init__(self, p, name, ncols, dtype=F16):
        self.p = p
        self.t = p.ctx.enter_context(p.nc.sbuf_tensor(name, [128, ncols], dtype))
        self.name = name
        self.ncols = ncols
        self.ep = 0

    def role(self, bcols):
        old = [k for k in list(self.p.res) if k.startswith(f"{self.name}.")]
        self.ep += 1
        nblk = self.ncols // bcols
        union = self.p.union_keys(old)
        for b in range(nblk):
            self.p.res[f"{self.name}.{self.ep}:{b}"] = {"w": None, "r": dict(union)}
        return RoleView(self, bcols, self.ep, None)


class RoleView:
    def __init__(self, buf, bcols, ep, bar):
        self.t = buf.t
        self.name = buf.name
        self.ncols = buf.ncols
        self.bcols = bcols
        self.ep = ep
        self.bar = bar

    def blk(self, b, c0=0, c1=None, p0=0, p1=128):
        c1 = self.bcols if c1 is None else c1
        return self.t[p0:p1, b * self.bcols + c0: b * self.bcols + c1]

    def view3(self):
        nb = self.ncols // self.bcols
        return self.t[:, 0:nb * self.bcols].rearrange("p (b c) -> p b c", c=self.bcols)

    def k(self, b):
        return f"{self.name}.{self.ep}:{b}"

    def rk(self, blocks):
        return [self.k(b) for b in blocks]

    def wk(self, blocks):
        return [self.k(b) for b in blocks]


def build_core_program(T=2048, upto=4, shard_wq=False):
    TT = T // 128
    QC = max(1, T // 512)
    CW = min(512, T)
    nc = bass.Bass("TRN2", target_bir_lowering=False)
    dram = {}

    def din(name, shape, dtype=F32):
        dram[name] = nc.dram_tensor(name, list(shape), dtype, kind="ExternalInput").ap()

    din("x", (T, D))
    dram["y"] = nc.dram_tensor("y", [T, D], F32, kind="ExternalOutput").ap()
    din("identb", (128, 128), F16)
    din("identf", (128, 128), F32)
    din("ones_b", (1, 128), F16)
    din("headmask", (128, FT * NH))
    din("trimask", (128, 128), F16)

    if upto >= 1:
        for w in ("Wq", "Wk", "Wv", "Wfq", "Wfk"):
            din(w, (D, D))
        din("W1", (D, MLP))
        din("W2", (MLP, D))
        din("bfq_col", (128, FT))
        din("b1_col", (128, MLP // 128))
        for v in ("gq", "betaq", "gk", "betak"):
            din(v + "_col", (128, FT))
        for v in ("gv", "betav", "ga", "ba"):
            din(v + "_bc", (128, D))
        for v in ("bq_row", "bk_row", "bv_row", "bfk_row", "b2_row"):
            din(v, (1, D))
        dram["w2scr"] = nc.dram_tensor("w2scr", [MLP, D], F16, kind="Internal").ap()
    if upto >= 2:
        din("w_gate", (FI, D))
        din("w_up", (FI, D))
        din("w_down", (D, FI))
        if shard_wq == "local":
            dram["wgoA"] = nc.dram_tensor("wgoA", [44 * 128, D], F16, kind="Internal").ap()
            dram["wgoB"] = nc.dram_tensor("wgoB", [8 * 128, FI], F16, kind="Internal").ap()
    if upto >= 3:
        din("awq", (D, D))
        din("awk", (D, D))
        din("awv", (D, D))
        din("out_W", (D, D))
        din("abq_col", (128, NH))
        din("abk_col", (128, NH))
        din("abv_row", (1, D))
        din("outb_row", (1, D))
        dram["x2scr"] = nc.dram_tensor("x2scr", [T, D], F16, kind="Internal").ap()
    if upto >= 4:
        din("W_r", (D, D))
        din("W_k", (D, D))
        din("W_v", (D, D))
        din("mu_r_col", (128, FT))
        din("mu_k_col", (128, FT))

    with contextlib.ExitStack() as ctx:
        p = Prog(nc, ctx)
        sb = lambda name, shape, dtype: ctx.enter_context(nc.sbuf_tensor(name, shape, dtype))

        NCOL = max(8 * T, 16384)
        G1 = Buf(p, "G1", NCOL)
        G2 = Buf(p, "G2", NCOL)
        G3 = Buf(p, "G3", NCOL)
        G4 = Buf(p, "G4", NCOL)
        GV = Buf(p, "GV", max(NCOL, TT * NH * 65))
        WAbB = Buf(p, "WAbB", 8 * 1024)
        ROT = Buf(p, "ROT", 2 * 1024).role(1024)
        xstage = sb("xstage", [128, D], F32)

        consts = {}
        for name, dt_ in (("identb", F16), ("identf", F32), ("ones_b", F16),
                          ("headmask", F32), ("trimask", F16)):
            consts[name] = sb(name + "_s", list(dram[name].shape), dt_)
            p.dma("sync", consts[name][:, :], dram[name][:, :], reads=[], writes=[name])

        STAT = sb("STAT", [128, 64], F32)
        QSC = sb("QSC", [128, 3072], F16)
        QG = sb("QG", [128, 640], F32)
        BR = sb("BR", [1, 2112], F16)
        DUM = sb("DUM", [128, 512], F16)
        KVS = sb("KVS", [128, 8 * 65], F16)
        QPC = sb("QPC", [128, FT], F32)
        QPBD = sb("QPBD", [128, FT * NH], F16)
        SR = sb("SR", [16, 4], F32)
        STc = sb("STc", [128, TT * NH], F32)
        COLS = {}

        PS = [ctx.enter_context(nc.psum_tensor(f"ps{i}", [128, 512], F32)) for i in range(6)]
        PSB = [ctx.enter_context(nc.psum_tensor(f"psb{i}", [128, 1024], F16)) for i in range(2)]
        psk = [f"ps:{i}" for i in range(6)]
        psbk = [f"psb:{i}" for i in range(2)]

        def load_cols(name, shape=None, dtype=F32):
            shape = shape or list(dram[name].shape)
            t = sb(name + "_s", shape, dtype)
            COLS[name] = t
            p.dma("sync", t[:, :], dram[name][:, :], reads=[], writes=[name])
            return t

        def load_row(name, slot):
            c0 = 0 if slot == 0 else 1056
            p.dma("gpsimd", BR[0:1, c0:c0 + D], dram[name][0:1, :],
                  reads=[], writes=[f"brow:{slot}"])
            return BR[0:1, c0:c0 + D]

        BCT = [sb("bct0", [128, D], F16), sb("bct1", [128, D], F16)]

        def cast_bc(name, slot):
            p.dma("sync", xstage[:, 0:D], dram[name][:, :], reads=[], writes=["xst:0"])
            p.op("scalar", ["xst:0"], [f"bct:{slot}"],
                 lambda h, slot=slot: h.copy(BCT[slot][:, :], xstage[:, 0:D]))
            return BCT[slot]

        def load_weight(dst: Buf, hbm, nrowblk, cols=D, name=None):
            name = name or hbm
            for rb in range(nrowblk):
                p.dma("gpsimd", dst.blk(rb, 0, cols),
                      dram[hbm][rb * 128:(rb + 1) * 128, 0:cols],
                      reads=[], writes=dst.wk([rb]))

        def mm_chain(psbanks, lhs_list, rhs_list, reads, extra=None):
            nb = len(psbanks)

            def fn(h):
                last = None
                nchain = len(lhs_list)
                for j in range(nchain):
                    for i in range(nb):
                        last = h.matmul(psbanks[i], lhs_list[j], rhs_list[j][i],
                                        start=(j == 0), stop=(j == nchain - 1 and not extra))
                if extra:
                    for idx, (el, er) in enumerate(extra):
                        for i in range(nb):
                            last = h.matmul(psbanks[i], el, er[i],
                                            start=False,
                                            stop=(idx == len(extra) - 1))
                return last
            return fn

        def phase0():
            XT = G1.role(T)
            for tt in range(TT):
                sl = tt % 2
                p.dma("gpsimd", ROT.blk(sl),
                      dram["x"][tt * 128:(tt + 1) * 128, :], reads=[], writes=ROT.wk([sl]))
                transpose_tile_to_T(ROT, sl, XT, tt)
            return XT

        tpev = [0]

        def transpose_tile_to_T(src, sblk, dstT, tt, nft=FT, f0=0):
            done = 0
            bk = 0
            while done < nft:
                nb = min(4, nft - done)
                bk = 1 - bk

                def mk(h, sblk=sblk, done=done, nb=nb, bk=bk):
                    last = None
                    for j in range(nb):
                        ft = done + j
                        last = h.transpose(PSB[bk][:, j * 128:(j + 1) * 128],
                                           src.blk(sblk, ft * 128, (ft + 1) * 128),
                                           consts["identb"][:, :])
                    return last
                p.op("tensor", [src.k(sblk), "identb"], [psbk[bk]], mk)
                tpev[0] += 1
                eng = "scalar" if tpev[0] % 2 == 0 else "vector"

                def ev(h, done=done, nb=nb, bk=bk, tt=tt, f0=f0, eng=eng):
                    dst = dstT.view3()[:, f0 + done:f0 + done + nb, tt * 128:tt * 128 + 128]
                    srcv = PSB[bk][:, 0:nb * 128].rearrange("p (b c) -> p b c", c=128)
                    return h.copy(dst, srcv) if eng == "scalar" else h.tensor_copy(dst, srcv)
                p.op(eng, [psbk[bk]], dstT.wk(range(f0 + done, f0 + done + nb)), ev)
                done += nb

        def ln_stats_math(slot):
            b0 = slot * 8
            sk = lambda i: f"st{slot}:{i}"
            p.op("vector", [sk(0), sk(1)], [sk(4)],
                 lambda h, b0=b0: h.tensor_tensor(STAT[:, b0 + 4:b0 + 5], STAT[:, b0:b0 + 1],
                                                  STAT[:, b0 + 1:b0 + 2], op=ALU.add))
            p.op("vector", [sk(4)], [sk(4)],
                 lambda h, b0=b0: h.tensor_scalar(STAT[:, b0 + 4:b0 + 5], STAT[:, b0 + 4:b0 + 5],
                                                  1.0 / D, None, op0=ALU.mult))
            p.op("vector", [sk(2), sk(3)], [sk(5)],
                 lambda h, b0=b0: h.tensor_tensor(STAT[:, b0 + 5:b0 + 6], STAT[:, b0 + 2:b0 + 3],
                                                  STAT[:, b0 + 3:b0 + 4], op=ALU.add))
            p.op("vector", [sk(5)], [sk(5)],
                 lambda h, b0=b0: h.tensor_scalar(STAT[:, b0 + 5:b0 + 6], STAT[:, b0 + 5:b0 + 6],
                                                  1.0 / D, None, op0=ALU.mult))
            p.op("vector", [sk(4)], [sk(6)],
                 lambda h, b0=b0: h.tensor_tensor(STAT[:, b0 + 6:b0 + 7], STAT[:, b0 + 4:b0 + 5],
                                                  STAT[:, b0 + 4:b0 + 5], op=ALU.mult))
            p.op("vector", [sk(5), sk(6)], [sk(5)],
                 lambda h, b0=b0: h.tensor_tensor(STAT[:, b0 + 5:b0 + 6], STAT[:, b0 + 5:b0 + 6],
                                                  STAT[:, b0 + 6:b0 + 7], op=ALU.subtract))
            p.op("vector", [sk(5)], [sk(5)],
                 lambda h, b0=b0: h.tensor_scalar(STAT[:, b0 + 5:b0 + 6], STAT[:, b0 + 5:b0 + 6],
                                                  float(EPS), None, op0=ALU.add))
            p.op("scalar", [sk(5)], [sk(7)],
                 lambda h, b0=b0: h.activation(STAT[:, b0 + 7:b0 + 8], STAT[:, b0 + 5:b0 + 6],
                                               AF.Sqrt))
            p.op("vector", [sk(7)], [sk(7)],
                 lambda h, b0=b0: h.reciprocal(STAT[:, b0 + 7:b0 + 8], STAT[:, b0 + 7:b0 + 8]))
            return (STAT[:, b0 + 4:b0 + 5], STAT[:, b0 + 7:b0 + 8], sk(4), sk(7))

        def phase1(XT):
            load_cols("bfq_col")
            p.op("vector", ["bfq_col"], ["bfq_col"],
                 lambda h: h.tensor_scalar(COLS["bfq_col"][:, :], COLS["bfq_col"][:, :],
                                           2.0, None, op0=ALU.mult))
            load_cols("b1_col")
            for v in ("gq", "betaq", "gk", "betak"):
                load_cols(v + "_col")
            bct = {"gv": cast_bc("gv_bc", 0), "betav": cast_bc("betav_bc", 1)}

            def proj_plain(Wbuf, brow, brkey, lnf):
                for tt in range(TT):
                    b0, b1_ = PS[(tt % 3) * 2], PS[(tt % 3) * 2 + 1]
                    lhs = [XT.blk(ft, tt * 128, tt * 128 + 128) for ft in range(FT)]
                    rhs = [[Wbuf.blk(ft, 0, 512), Wbuf.blk(ft, 512, 1024)] for ft in range(FT)]
                    extra = [(consts["ones_b"][0:1, 0:128],
                              [brow[0:1, 0:512], brow[0:1, 512:1024]])]
                    p.op("tensor",
                         XT.rk(range(FT)) + [Wbuf.k(ft) for ft in range(FT)] + ["ones_b", brkey],
                         [psk[(tt % 3) * 2], psk[(tt % 3) * 2 + 1]],
                         mm_chain([b0[:, :], b1_[:, :]], lhs, rhs, None, extra))
                    lnf(tt, (tt % 3) * 2)

            def ln_evac(tt, bki, dst, dstblk, gaincols=None, gbc=None, strided65=False):
                sl = tt % 2
                slot = tt % 4
                b0 = slot * 8
                sk = lambda i_: f"st{slot}:{i_}"
                p.op("scalar", [psk[bki]], ROT.wk([sl]) + [sk(0)],
                     lambda h, b0=b0: h.activation(ROT.blk(sl, 0, 512), PS[bki][:, :], AF.Copy,
                                                   accum_out=STAT[:, b0:b0 + 1]))
                p.op("scalar", [psk[bki + 1]], ROT.wk([sl]) + [sk(1)],
                     lambda h, b0=b0: h.activation(ROT.blk(sl, 512, 1024), PS[bki + 1][:, :],
                                                   AF.Copy, accum_out=STAT[:, b0 + 1:b0 + 2]))
                p.op("scalar", [psk[bki]], [sk(2), "DUMk"],
                     lambda h, b0=b0: h.activation(DUM[:, :], PS[bki][:, :], AF.Square,
                                                   accum_out=STAT[:, b0 + 2:b0 + 3]))
                p.op("scalar", [psk[bki + 1]], [sk(3), "DUMk"],
                     lambda h, b0=b0: h.activation(DUM[:, :], PS[bki + 1][:, :], AF.Square,
                                                   accum_out=STAT[:, b0 + 3:b0 + 4]))
                mAP, rAP, mk_, rk_ = ln_stats_math(slot)
                if gbc is None:
                    p.op("vector", [ROT.k(sl), mk_, rk_], dst.wk([dstblk]),
                         lambda h, mAP=mAP, rAP=rAP: h.tensor_scalar(
                             dst.blk(dstblk), ROT.blk(sl), mAP, rAP,
                             op0=ALU.subtract, op1=ALU.mult))
                else:
                    gt, bt = gbc
                    p.op("vector", [ROT.k(sl), mk_, rk_], ROT.wk([sl]),
                         lambda h, mAP=mAP, rAP=rAP: h.tensor_scalar(
                             ROT.blk(sl), ROT.blk(sl), mAP, rAP,
                             op0=ALU.subtract, op1=ALU.mult))
                    p.op("vector", [ROT.k(sl), "bct:0"], ROT.wk([sl]),
                         lambda h: h.tensor_tensor(ROT.blk(sl), ROT.blk(sl), gt[:, :], op=ALU.mult))
                    if not strided65:
                        p.op("vector", [ROT.k(sl), "bct:1"], dst.wk([dstblk]),
                             lambda h: h.tensor_tensor(dst.blk(dstblk), ROT.blk(sl), bt[:, :], op=ALU.add))
                    else:
                        def ev(h):
                            dv = dst.blk(dstblk).rearrange("p (g c) -> p g c", c=65)[:, :, 0:64]
                            sv = ROT.blk(sl).rearrange("p (g c) -> p g c", c=64)
                            return h.tensor_tensor(dv, sv, bt[:, :].rearrange("p (g c) -> p g c", c=64), op=ALU.add)
                        p.op("vector", [ROT.k(sl), "bct:1"], dst.wk([dstblk]), ev)

            V65 = GV.role(NH * 65)
            p.op("vector", [], V65.wk(range(TT)),
                 lambda h: h.memset(V65.t[:, 0:TT * NH * 65], 1.0))
            Wv = WAbB.role(1024)
            load_weight(Wv, "Wv", FT)
            bv_r = load_row("bv_row", 0)
            proj_plain(Wv, bv_r, "brow:0",
                       lambda tt, bki: ln_evac(tt, bki, V65, tt,
                                               gbc=(bct["gv"], bct["betav"]), strided65=True))
            bg_drain(4)

            QT = G2.role(T)
            QPL = G4.role(D)
            Wq = WAbB.role(1024)
            load_weight(Wq, "Wq", FT)
            bq_r = load_row("bq_row", 0)

            def q_evac(tt, bki):
                ln_evac(tt, bki, QPL, tt, gaincols=None)
            proj_plain(Wq, bq_r, "brow:0", q_evac)
            bg_drain(4)
            for tt in range(TT):
                transpose_tile_to_T(QPL, tt, QT, tt)
            for ft in range(FT):
                p.op("vector", [QT.k(ft), "gq_col", "betaq_col"], QT.wk([ft]),
                     lambda h, ft=ft: h.tensor_scalar(QT.blk(ft), QT.blk(ft),
                                                      COLS["gq_col"][:, ft:ft + 1],
                                                      COLS["betaq_col"][:, ft:ft + 1],
                                                      op0=ALU.mult, op1=ALU.add))

            for ft in range(FT):
                p.op("vector", [QT.k(ft)], [f"qpc:{ft}"],
                     lambda h, ft=ft: h.reduce_sum(QPC[:, ft:ft + 1], QT.blk(ft), axis=AX.X))
                p.op("vector", [f"qpc:{ft}", "headmask"], [f"qpbd:{ft}"],
                     lambda h, ft=ft: h.tensor_scalar(
                         QPBD[:, ft * NH:(ft + 1) * NH],
                         consts["headmask"][:, ft * NH:(ft + 1) * NH],
                         QPC[:, ft:ft + 1], 1.0 / (T * 8.0),
                         op0=ALU.mult, op1=ALU.mult))

            PQT = G3.role(T)
            Wfq = WAbB.role(1024)
            load_weight(Wfq, "Wfq", FT)
            for dt in range(FT):
                for qc in range(QC):
                    bk = (dt * QC + qc) % 3
                    lhs = [Wfq.blk(ft, dt * 128, dt * 128 + 128) for ft in range(FT)]
                    rhs = [[QT.blk(ft, qc * CW, qc * CW + CW)] for ft in range(FT)]
                    p.op("tensor", [Wfq.k(ft) for ft in range(FT)] + QT.rk(range(FT)),
                         [psk[bk]],
                         mm_chain([PS[bk][:, 0:CW]], lhs, rhs, None))
                    p.op("scalar", [psk[bk], "bfq_col"], PQT.wk([dt]),
                         lambda h, dt=dt, qc=qc, bk=bk: h.activation(
                             PQT.blk(dt, qc * CW, qc * CW + CW), PS[bk][:, 0:CW], AF.Sigmoid,
                             bias=COLS["bfq_col"][:, dt:dt + 1], scale=2.0))

            KT = G2.role(T)
            QPL = G4.role(D)
            Wk = WAbB.role(1024)
            load_weight(Wk, "Wk", FT)
            bk_r = load_row("bk_row", 0)

            def k_evac(tt, bki):
                ln_evac(tt, bki, QPL, tt)
            proj_plain(Wk, bk_r, "brow:0", k_evac)
            bg_drain(4)
            for tt in range(TT):
                transpose_tile_to_T(QPL, tt, KT, tt)
            for ft in range(FT):
                p.op("vector", [KT.k(ft), "gk_col", "betak_col"], KT.wk([ft]),
                     lambda h, ft=ft: h.tensor_scalar(KT.blk(ft), KT.blk(ft),
                                                      COLS["gk_col"][:, ft:ft + 1],
                                                      COLS["betak_col"][:, ft:ft + 1],
                                                      op0=ALU.mult, op1=ALU.add))

            bg_drain(4)
            for qc in range(QC):
                bk = qc % 3
                lhs = [QPBD[:, ft * NH:(ft + 1) * NH] for ft in range(FT)]
                rhs = [[KT.blk(ft, qc * CW, qc * CW + CW)] for ft in range(FT)]
                p.op("tensor", [f"qpbd:{ft}" for ft in range(FT)] + KT.rk(range(FT)),
                     [psk[bk]],
                     mm_chain([PS[bk][0:16, 0:CW]], lhs, rhs, None))
                p.op("scalar", [psk[bk]], [f"G4.SS:{qc}"],
                     lambda h, qc=qc, bk=bk: h.activation(G4.t[0:16, qc * CW:qc * CW + CW],
                                                          PS[bk][0:16, 0:CW], AF.Exp))
            PHIK = G1.role(D)
            Wfk = WAbB.role(1024)
            load_weight(Wfk, "Wfk", FT)
            bfk_r = load_row("bfk_row", 1)
            for tt in range(TT):
                bki = (tt % 3) * 2
                lhs = [KT.blk(ft, tt * 128, tt * 128 + 128) for ft in range(FT)]
                rhs = [[Wfk.blk(ft, 0, 512), Wfk.blk(ft, 512, 1024)] for ft in range(FT)]
                extra = [(consts["ones_b"][0:1, 0:128],
                          [bfk_r[0:1, 0:512], bfk_r[0:1, 512:1024]])]
                p.op("tensor", KT.rk(range(FT)) + [Wfk.k(ft) for ft in range(FT)] + ["ones_b", "brow:1"],
                     [psk[bki], psk[bki + 1]],
                     mm_chain([PS[bki][:, :], PS[bki + 1][:, :]], lhs, rhs, None, extra))
                for half in range(2):
                    p.op("scalar", [psk[bki + half]], [f"phikr:{tt}:{half}"],
                         lambda h, tt=tt, bki=bki, half=half: h.activation(
                             PHIK.blk(tt, half * 512, half * 512 + 512),
                             PS[bki + half][:, :], AF.Sigmoid, scale=2.0))
            p.op("vector", [f"G4.SS:{qc}" for qc in range(QC)], ["srden"],
                 lambda h: h.reduce_sum(SR[0:16, 0:1], G4.t[0:16, 0:T], axis=AX.X))
            p.op("vector", ["srden"], ["srden"],
                 lambda h: h.tensor_scalar(SR[0:16, 0:1], SR[0:16, 0:1], 1.0, None, op0=ALU.add))
            p.op("vector", ["srden"], ["s0"],
                 lambda h: h.reciprocal(SR[0:16, 1:2], SR[0:16, 0:1]))
            p.op("vector", [f"G4.SS:{qc}" for qc in range(QC)] + ["s0"], ["G4.SSn"],
                 lambda h: h.tensor_scalar(G4.t[0:16, 0:T], G4.t[0:16, 0:T], SR[0:16, 1:2],
                                           None, op0=ALU.mult))
            for tt in range(TT):
                bk2 = tt % 2

                def mkt(h, tt=tt, bk2=bk2):
                    return h.transpose(PSB[bk2][0:128, 0:16],
                                       G4.t[0:16, tt * 128:(tt + 1) * 128],
                                       consts["identb"][0:16, 0:16])
                p.op("tensor", ["G4.SSn", "identf"], [psbk[bk2]], mkt)
                p.op("vector", [psbk[bk2]], [f"stc:{tt}"],
                     lambda h, tt=tt, bk2=bk2: h.tensor_copy(
                         STc[:, tt * NH:(tt + 1) * NH], PSB[bk2][0:128, 0:16]))

            for tt in range(TT):
                p.op("vector", [f"phikr:{tt}:0", f"phikr:{tt}:1", f"stc:{tt}"], PHIK.wk([tt]),
                     lambda h, tt=tt: h.tensor_tensor(
                         PHIK.blk(tt).rearrange("p (g c) -> p g c", c=64),
                         PHIK.blk(tt).rearrange("p (g c) -> p g c", c=64),
                         STc[:, tt * NH:(tt + 1) * NH].unsqueeze(2).broadcast_to([128, NH, 64]),
                         op=ALU.mult))

            p.op("scalar", ["brow:1"], ["brow:1"],
                 lambda h: h.activation(BR[0:1, 1056:1056 + D], BR[0:1, 1056:1056 + D],
                                        AF.Sigmoid, scale=2.0))
            p.op("tensor", ["s0", "identf"], [psk[5]],
                 lambda h: h.transpose(PS[5][0:1, 0:16], SR[0:16, 1:2],
                                       consts["identf"][0:16, 0:16]))
            p.op("vector", [psk[5]], ["s0r"],
                 lambda h: h.tensor_copy(STAT[0:1, 48:64], PS[5][0:1, 0:16]))
            p.op("vector", [], ["brow:0"],
                 lambda h: h.memset(BR[0:1, 0:NH * 65], 0.0))
            p.op("vector", ["s0r"], ["brow:0"],
                 lambda h: h.tensor_copy(
                     BR[0:1, 0:NH * 65].rearrange("p (g c) -> p g c", c=65)[:, :, 64:65],
                     STAT[0:1, 48:64].rearrange("p (g c) -> p g c", c=1)))

            bg_drain(4)
            for h_ in range(NH):
                blk, base = h_ // 2, (h_ % 2) * 64
                bk = h_ % 2

                def mkv(h, h_=h_, blk=blk, base=base, bk=bk):
                    last = None
                    for tt in range(TT):
                        last = h.matmul(PS[bk][base:base + 64, 0:65],
                                        PHIK.blk(tt, h_ * 64, h_ * 64 + 64),
                                        V65.blk(tt, h_ * 65, (h_ + 1) * 65),
                                        start=(tt == 0), stop=False)
                    last = h.matmul(PS[bk][base:base + 64, 0:65],
                                    BR[0:1, 1056 + h_ * 64: 1056 + (h_ + 1) * 64],
                                    BR[0:1, h_ * 65: (h_ + 1) * 65],
                                    start=False, stop=True)
                    return last
                p.op("tensor", PHIK.rk(range(TT)) + V65.rk(range(TT)) + ["brow:1", "brow:0"],
                     [psk[bk]], mkv)
                p.op("vector", [psk[bk]], [f"kvs:{h_}"],
                     lambda h, h_=h_, blk=blk, base=base, bk=bk: h.tensor_copy(
                         KVS[base:base + 64, blk * 65:(blk + 1) * 65],
                         PS[bk][base:base + 64, 0:65]))

            A = G2.role(D)
            for tt in range(TT):
                grp = tt % 2
                pbase = 3 * grp
                for h_ in range(NH):
                    bi = pbase + (h_ // 7 if h_ < 14 else 2)
                    col = (h_ % 7) * 65 if h_ < 14 else (h_ - 14) * 65
                    blk, base = h_ // 2, (h_ % 2) * 64

                    def mnum(h, tt=tt, h_=h_, bi=bi, col=col, blk=blk, base=base):
                        return h.matmul(PS[bi][:, col:col + 65],
                                        PQT.blk(blk, tt * 128, tt * 128 + 128, p0=base, p1=base + 64),
                                        KVS[base:base + 64, blk * 65:(blk + 1) * 65],
                                        start=True, stop=True)
                    p.op("tensor", [PQT.k(blk), f"kvs:{h_}"], [psk[bi]], mnum)
                def dget(h, tt=tt, pbase=pbase):
                    return h.tensor_copy(
                        STAT[:, 8:24],
                        PS[pbase][:, 0:455].rearrange("p (g c) -> p g c", c=65)[:, :, 64:65])
                p.op("vector", [psk[pbase]], ["dcol"],
                     lambda h, pbase=pbase: h.tensor_copy(
                         STAT[:, 8:15],
                         PS[pbase][:, 0:455].rearrange("p (g c) -> p g c", c=65)[:, :, 64:65].rearrange("p g c -> p (g c)")))
                p.op("vector", [psk[pbase + 1]], ["dcol2"],
                     lambda h, pbase=pbase: h.tensor_copy(
                         STAT[:, 15:22],
                         PS[pbase + 1][:, 0:455].rearrange("p (g c) -> p g c", c=65)[:, :, 64:65].rearrange("p g c -> p (g c)")))
                p.op("vector", [psk[pbase + 2]], ["dcol3"],
                     lambda h, pbase=pbase: h.tensor_copy(
                         STAT[:, 22:24],
                         PS[pbase + 2][:, 0:130].rearrange("p (g c) -> p g c", c=65)[:, :, 64:65].rearrange("p g c -> p (g c)")))
                p.op("vector", ["dcol", "dcol2", "dcol3"], ["dcol"],
                     lambda h: h.tensor_scalar(STAT[:, 8:24], STAT[:, 8:24], 2.5e-7, None, op0=ALU.add))
                p.op("vector", ["dcol"], ["dcol"],
                     lambda h: h.reciprocal(STAT[:, 8:24], STAT[:, 8:24]))
                for bnum, (h0, nh) in enumerate(((0, 7), (7, 7), (14, 2))):
                    bi = pbase + bnum
                    p.op("vector", [psk[bi], "dcol"], A.wk([tt]),
                         lambda h, tt=tt, bi=bi, h0=h0, nh=nh: h.tensor_tensor(
                             A.blk(tt, h0 * 64, (h0 + nh) * 64).rearrange(
                                 "p (g c) -> p g c", c=64),
                             PS[bi][:, 0:nh * 65].rearrange("p (g c) -> p g c", c=65)[:, :, 0:64],
                             STAT[:, 8 + h0:8 + h0 + nh].unsqueeze(2).broadcast_to([128, nh, 64]),
                             op=ALU.mult))

            bg_drain(4)
            cast_bc("ga_bc", 0)
            cast_bc("ba_bc", 1)
            for tt in range(TT):
                slot = tt % 4
                b0 = slot * 8
                sk = lambda i_: f"st{slot}:{i_}"
                for i_, (c0, func) in enumerate(((0, AF.Copy), (512, AF.Copy),
                                                 (0, AF.Square), (512, AF.Square))):
                    p.op("scalar", [A.k(tt)], [sk(i_), "DUMk"],
                         lambda h, tt=tt, c0=c0, func=func, i_=i_, b0=b0: h.activation(
                             DUM[:, :], A.blk(tt, c0, c0 + 512), func,
                             accum_out=STAT[:, b0 + i_:b0 + i_ + 1]))
                mAP, rAP, mk_, rk_ = ln_stats_math(slot)
                p.op("vector", [A.k(tt), mk_, rk_], A.wk([tt]),
                     lambda h, tt=tt, mAP=mAP, rAP=rAP: h.tensor_scalar(
                         A.blk(tt), A.blk(tt), mAP, rAP, op0=ALU.subtract, op1=ALU.mult))
                p.op("vector", [A.k(tt), "bct:0"], A.wk([tt]),
                     lambda h, tt=tt: h.tensor_tensor(A.blk(tt), A.blk(tt), BCT[0][:, :], op=ALU.mult))
                p.op("vector", [A.k(tt), "bct:1"], A.wk([tt]),
                     lambda h, tt=tt: h.tensor_tensor(A.blk(tt), A.blk(tt), BCT[1][:, :], op=ALU.add))

            bg_drain(4)
            AT = G1.role(T)
            for tt in range(TT):
                transpose_tile_to_T(A, tt, AT, tt)

            for tt in range(TT):
                sl = tt % 2
                p.dma("gpsimd", ROT.blk(sl),
                      dram["x"][tt * 128:(tt + 1) * 128, :], reads=[], writes=ROT.wk([sl]))
                p.op("vector", [ROT.k(sl), A.k(tt)], A.wk([tt]),
                     lambda h, tt=tt, sl=sl: h.scalar_tensor_tensor(
                         A.blk(tt), ROT.blk(sl), 2.0, A.blk(tt),
                         op0=ALU.mult, op1=ALU.add))
            R = A

            W1a = G4.role(MLP)
            load_weight(W1a, "W1", 4, cols=MLP)
            W1b = GV.role(MLP)
            for rb in range(4):
                p.dma("gpsimd", W1b.blk(rb, 0, MLP),
                      dram["W1"][(rb + 4) * 128:(rb + 5) * 128, 0:MLP],
                      reads=[], writes=W1b.wk([rb]))
            W2s = WAbB.role(1024)
            for rb in range(MLP // 128):
                p.dma("gpsimd", W2s.blk(rb % 8),
                      dram["W2"][rb * 128:(rb + 1) * 128, :],
                      reads=[], writes=W2s.wk([rb % 8]))
                p.dma("sync", dram["w2scr"][rb * 128:(rb + 1) * 128, :], W2s.blk(rb % 8),
                      reads=[W2s.k(rb % 8)], writes=[f"w2scr:{rb}"])

            GCH = G3.role(512)
            W2r = WAbB.role(1024)
            b2_r = load_row("b2_row", 0)
            TC = max(1, T // 512)
            NK2 = MLP // 128
            wq_bct[0] = True
            for tc in range(TC):
                bg_drain(4)
                for k2 in range(NK2):
                    bk = k2 % 2
                    lhs = [W1a.blk(ft, k2 * 128, (k2 + 1) * 128) for ft in range(4)] + \
                          [W1b.blk(ft - 4, k2 * 128, (k2 + 1) * 128) for ft in range(4, 8)]
                    rhs = [[AT.blk(ft, tc * CW, tc * CW + CW)] for ft in range(FT)]
                    p.op("tensor",
                         [W1a.k(f) for f in range(4)] + [W1b.k(f) for f in range(4)] + AT.rk(range(FT)),
                         [psk[bk]], mm_chain([PS[bk][:, 0:CW]], lhs, rhs, None))
                    p.op("scalar", [psk[bk], "b1_col"], GCH.wk([k2]),
                         lambda h, k2=k2, bk=bk: h.activation(
                             GCH.blk(k2, 0, CW), PS[bk][:, 0:CW], AF.Gelu,
                             bias=COLS["b1_col"][:, k2:k2 + 1]))
                ntt = CW // 128
                for jj in range(ntt // 2):
                    for g in range(NK2 // 8):
                        for kk in range(8):
                            k2 = g * 8 + kk
                            p.dma("sync", W2r.blk(kk), dram["w2scr"][k2 * 128:(k2 + 1) * 128, :],
                                  reads=[f"w2scr:{k2}"], writes=W2r.wk([kk]))
                        for jl in range(2):
                            j = jj * 2 + jl
                            bb = 2 + jl * 2

                            def mdown(h, j=j, g=g, bb=bb, last_g=(g == NK2 // 8 - 1)):
                                last = None
                                for kk in range(8):
                                    k2 = g * 8 + kk
                                    for half in range(2):
                                        last = h.matmul(
                                            PS[bb + half][:, :],
                                            GCH.blk(k2, j * 128, (j + 1) * 128),
                                            W2r.blk(kk, half * 512, half * 512 + 512),
                                            start=(g == 0 and kk == 0), stop=False)
                                if last_g:
                                    for half in range(2):
                                        last = h.matmul(PS[bb + half][:, :],
                                                        consts["ones_b"][0:1, 0:128],
                                                        b2_r[0:1, half * 512:half * 512 + 512],
                                                        start=False, stop=True)
                                return last
                            p.op("tensor",
                                 GCH.rk(range(g * 8, g * 8 + 8)) + [W2r.k(kk) for kk in range(8)] +
                                 ["ones_b", "brow:0"],
                                 [psk[bb], psk[bb + 1]], mdown)
                    for jl in range(2):
                        j = jj * 2 + jl
                        tt = tc * ntt + j
                        bb = 2 + jl * 2
                        sl = tt % 2
                        for half in range(2):
                            p.op("vector", [psk[bb + half], R.k(tt)], ROT.wk([sl]),
                                 lambda h, tt=tt, half=half, sl=sl, bb=bb: h.tensor_tensor(
                                     ROT.blk(sl, half * 512, half * 512 + 512),
                                     PS[bb + half][:, :],
                                     R.blk(tt, half * 512, half * 512 + 512), op=ALU.add))
                        ln_plain_tile(ROT, sl, R, tt)
                    bg_drain(3)
            bg_drain()
            wq_bct[0] = False

            return R

        def ln_plain_tile(src, sblk, dst, dblk, out_f32_stage=None):
            slot = dblk % 4
            b0 = slot * 8
            sk = lambda i_: f"st{slot}:{i_}"
            for i_, (c0, func) in enumerate(((0, AF.Copy), (512, AF.Copy),
                                             (0, AF.Square), (512, AF.Square))):
                p.op("scalar", [src.k(sblk)], [sk(i_), "DUMk"],
                     lambda h, c0=c0, func=func, i_=i_, b0=b0: h.activation(
                         DUM[:, :], src.blk(sblk, c0, c0 + 512), func,
                         accum_out=STAT[:, b0 + i_:b0 + i_ + 1]))
            mAP, rAP, mk_, rk_ = ln_stats_math(slot)
            p.op("vector", [src.k(sblk), mk_, rk_], dst.wk([dblk]),
                 lambda h, mAP=mAP, rAP=rAP: h.tensor_scalar(
                     dst.blk(dblk), src.blk(sblk), mAP, rAP,
                     op0=ALU.subtract, op1=ALU.mult))

        def quant_qd(srcap, srck, dstap, dstk, W):
            NG = W // 16
            MIDS = (0.25, 0.75, 1.25, 1.75, 2.5, 3.5, 5.0)
            sv = srcap.rearrange("p (g c) -> p g c", c=16)
            A = QSC[:, 0:W]
            Av = A.rearrange("p (g c) -> p g c", c=16)
            Q = QSC[:, 1024:1024 + W]
            Qv = Q.rearrange("p (g c) -> p g c", c=16)
            C = QSC[:, 2048:2048 + W]
            Cv = C.rearrange("p (g c) -> p g c", c=16)
            SG = dstap.bitcast(F16)
            g6b = QG[:, 64:64 + NG].unsqueeze(2).broadcast_to([128, NG, 16])
            WTS = (None, None, None, None, 2.0, 2.0, 4.0)

            r6b = QG[:, 128:128 + NG].unsqueeze(2).broadcast_to([128, NG, 16])
            p.op("vector", [srck], ["qg"],
                 lambda h: h.tensor_reduce(QG[:, 0:NG], sv, axis=AX.X, op=ALU.max,
                                           apply_absolute_value=True))
            p.op("vector", ["qg"], ["qg"],
                 lambda h: h.tensor_scalar(QG[:, 0:NG], QG[:, 0:NG], 6e-12, None, op0=ALU.max))
            p.op("vector", ["qg"], ["qg3"],
                 lambda h: h.tensor_scalar(QG[:, 64:64 + NG], QG[:, 0:NG], 1.0 / 6.0, None,
                                           op0=ALU.mult))
            def emit_sg():
                p.op("vector", [srck], [dstk],
                     lambda h: h.tensor_scalar(SG, srcap, 0.0, 0.5, op0=ALU.is_gt,
                                               op1=ALU.subtract))
            if srcap.dtype == F32:
                p.op("vector", ["qg3"], ["qt0"],
                     lambda h: h.reciprocal(QG[:, 128:128 + NG], QG[:, 64:64 + NG]))
                emit_sg()
                p.op("scalar", [srck], [srck],
                     lambda h: h.activation(srcap, srcap, AF.Abs))
                p.op("vector", [srck, "qt0"], [srck],
                     lambda h: h.tensor_tensor(sv, sv, r6b, op=ALU.mult))
                cmp_reads = [srck]

                def cmp(dstf, dstv, j):
                    if WTS[j] is None:
                        return lambda h: h.tensor_scalar(dstf, srcap, MIDS[j], None,
                                                         op0=ALU.is_gt)
                    return lambda h: h.tensor_scalar(dstf, srcap, MIDS[j], WTS[j],
                                                     op0=ALU.is_gt, op1=ALU.mult)
                cmp_extra = lambda j: []
            else:
                p.op("vector", ["qg3"], ["qt0"],
                     lambda h: h.reciprocal(QG[:, 128:128 + NG], QG[:, 64:64 + NG]))
                p.op("scalar", [srck], ["qA"],
                     lambda h: h.activation(A, srcap, AF.Abs))
                emit_sg()
                p.op("vector", ["qA", "qt0"], ["qA"],
                     lambda h: h.tensor_tensor(Av, Av, r6b, op=ALU.mult))
                cmp_reads = ["qA"]

                def cmp(dstf, dstv, j):
                    if WTS[j] is None:
                        return lambda h: h.tensor_scalar(dstf, A, MIDS[j], None,
                                                         op0=ALU.is_gt)
                    return lambda h: h.tensor_scalar(dstf, A, MIDS[j], WTS[j],
                                                     op0=ALU.is_gt, op1=ALU.mult)
                cmp_extra = lambda j: []
            p.op("vector", cmp_reads + cmp_extra(0), ["qQ"], cmp(Q, Qv, 0))
            for j in (1, 2, 3, 4, 5, 6):
                p.op("vector", cmp_reads + cmp_extra(j), ["qC"], cmp(C, Cv, j))
                p.op("vector", ["qQ", "qC"], ["qQ"],
                     lambda h: h.tensor_tensor(Q, Q, C, op=ALU.add))
            p.op("vector", ["qQ", dstk], ["qQ"],
                 lambda h: h.tensor_tensor(Q, Q, SG, op=ALU.mult))
            p.op("vector", ["qQ", "qg3", srck], [dstk],
                 lambda h: h.tensor_tensor(dstap.rearrange("p (g c) -> p g c", c=16), Qv, g6b,
                                           op=ALU.mult))

        def phase2(X1, XQTslot, OUTslot, W1slot, WVslot):
            QTR = [(0, 768), (768, 768), (1536, 768), (2304, 512)]
            XQT = XQTslot.role(T)
            for tt in range(TT):
                sl = tt % 2
                quant_qd(X1.blk(tt), X1.k(tt), ROT.blk(sl), ROT.k(sl), D)
                transpose_tile_to_T(ROT, sl, XQT, tt)
            OUT = OUTslot.role(D)
            for qi, (r0, RQ) in enumerate(QTR):
                NB = RQ // 128
                nch = (RQ + 511) // 512
                WGT = W1slot.role(RQ)
                for half, wsrc, toff in ((0, "w_gate", 0), (1, "w_up", FI // 128)):
                    WST = WAbB.role(1024)
                    for wb in range(NB):
                        t_ = toff + r0 // 128 + wb
                        if shard_wq == "local":
                            p.dma("sync", WST.blk(wb), dram["wgoA"][t_ * 128:(t_ + 1) * 128, :],
                                  reads=[f"wgo:False:{t_ * 128}:0"], writes=WST.wk([wb]))
                        else:
                            p.dma("sync", xstage[:, 0:D],
                                  dram[wsrc][r0 + wb * 128:r0 + (wb + 1) * 128, :],
                                  reads=["xst:0"], writes=["xst:0"])
                            quant_qd(xstage[:, 0:D], "xst:0", WST.blk(wb), WST.k(wb), D)
                    for wb in range(NB):
                        transpose_tile_to_T(WST, wb, WGT, wb, f0=half * FT)
                WDS = WAbB.role(RQ)
                for rb in range(FT):
                    if shard_wq == "local":
                        rdk = [f"wgo:True:{rb * 128}:{cc}" for cc in (0, 1024, 2048)]
                        p.dma("sync", WDS.blk(rb), dram["wgoB"][rb * 128:(rb + 1) * 128, r0:r0 + RQ],
                              reads=rdk, writes=WDS.wk([rb]))
                    else:
                        p.dma("sync", xstage[:, 0:RQ],
                              dram["w_down"][rb * 128:(rb + 1) * 128, r0:r0 + RQ],
                              reads=["xst:0"], writes=["xst:0"])
                        quant_qd(xstage[:, 0:RQ], "xst:0", WDS.blk(rb), WDS.k(rb), RQ)
                GT = WVslot.role(RQ)

                def gate_mm(tt):
                    bks = [(tt % 2) * 2, (tt % 2) * 2 + 1][:nch]
                    lhs = [XQT.blk(ft, tt * 128, tt * 128 + 128) for ft in range(FT)]
                    rhs = [[WGT.blk(ft, c * 512, min(RQ, (c + 1) * 512)) for c in range(nch)]
                           for ft in range(FT)]
                    p.op("tensor", XQT.rk(range(FT)) + WGT.rk(range(FT)),
                         [psk[b] for b in bks],
                         mm_chain([PS[bks[c]][:, 0:min(512, RQ - c * 512)] for c in range(nch)],
                                  lhs, rhs, None))
                    for c in range(nch):
                        w0, w1_ = c * 512, min(RQ, (c + 1) * 512)
                        p.op("scalar", [psk[bks[c]]], GT.wk([tt]),
                             lambda h, tt=tt, w0=w0, w1_=w1_, b=bks[c], GT=GT: h.activation(
                                 GT.blk(tt, w0, w1_), PS[b][:, 0:w1_ - w0], AF.Silu))

                def up_mm(tt):
                    bks = [4, 5][:nch]
                    lhs = [XQT.blk(ft, tt * 128, tt * 128 + 128) for ft in range(FT)]
                    rhs = [[WGT.blk(FT + ft, c * 512, min(RQ, (c + 1) * 512)) for c in range(nch)]
                           for ft in range(FT)]
                    p.op("tensor", XQT.rk(range(FT)) + WGT.rk(range(FT, 2 * FT)),
                         [psk[b] for b in bks],
                         mm_chain([PS[bks[c]][:, 0:min(512, RQ - c * 512)] for c in range(nch)],
                                  lhs, rhs, None))
                    for c in range(nch):
                        w0, w1_ = c * 512, min(RQ, (c + 1) * 512)
                        p.op("vector", [psk[bks[c]], GT.k(tt)], GT.wk([tt]),
                             lambda h, tt=tt, w0=w0, w1_=w1_, b=bks[c], GT=GT: h.tensor_tensor(
                                 GT.blk(tt, w0, w1_), PS[b][:, 0:w1_ - w0],
                                 GT.blk(tt, w0, w1_), op=ALU.mult))
                    quant_qd(GT.blk(tt, 0, RQ), GT.k(tt), GT.blk(tt, 0, RQ), GT.k(tt), RQ)

                gate_mm(0)
                for tt in range(1, TT):
                    gate_mm(tt)
                    up_mm(tt - 1)
                up_mm(TT - 1)
                HQT = W1slot.role(T)
                for tt in range(TT):
                    transpose_tile_to_T(GT, tt, HQT, tt, nft=NB)
                WDT = WVslot.role(1024)
                for rb in range(FT):
                    transpose_tile_to_T(WDS, rb, WDT, rb, nft=NB)
                for tt in range(TT):
                    bks = [4, 5]
                    lhs = [HQT.blk(k2, tt * 128, tt * 128 + 128) for k2 in range(NB)]
                    rhs = [[WDT.blk(k2, 0, 512), WDT.blk(k2, 512, 1024)] for k2 in range(NB)]
                    p.op("tensor", HQT.rk(range(NB)) + WDT.rk(range(NB)),
                         [psk[4], psk[5]],
                         mm_chain([PS[4][:, :], PS[5][:, :]], lhs, rhs, None))
                    for c in range(2):
                        if qi == 0:
                            p.op("scalar", [psk[4 + c]], OUT.wk([tt]),
                                 lambda h, tt=tt, c=c: h.activation(
                                     OUT.blk(tt, c * 512, (c + 1) * 512), PS[4 + c][:, :],
                                     AF.Copy))
                        else:
                            p.op("vector", [psk[4 + c], OUT.k(tt)], OUT.wk([tt]),
                                 lambda h, tt=tt, c=c: h.tensor_tensor(
                                     OUT.blk(tt, c * 512, (c + 1) * 512), PS[4 + c][:, :],
                                     OUT.blk(tt, c * 512, (c + 1) * 512), op=ALU.add))
            for tt in range(TT):
                sl = tt % 2
                p.op("vector", [X1.k(tt), OUT.k(tt)], ROT.wk([sl]),
                     lambda h, tt=tt, sl=sl: h.tensor_tensor(
                         ROT.blk(sl), X1.blk(tt), OUT.blk(tt), op=ALU.add))
                ln_plain_tile(ROT, sl, X1, tt)
            return X1

        def phase3(X2):
            for tt in range(TT):
                p.dma("sync", dram["x2scr"][tt * 128:(tt + 1) * 128, :], X2.blk(tt),
                      reads=[X2.k(tt)], writes=[f"x2scr:{tt}"])
            X2T = G1.role(T)
            for tt in range(TT):
                transpose_tile_to_T(X2, tt, X2T, tt)
            load_cols("abq_col")
            load_cols("abk_col")

            AWQK = G3.role(1024)
            for ft in range(FT):
                p.dma("gpsimd", AWQK.blk(ft), dram["awq"][ft * 128:(ft + 1) * 128, :],
                      reads=[], writes=AWQK.wk([ft]))
                p.dma("gpsimd", AWQK.blk(8 + ft), dram["awk"][ft * 128:(ft + 1) * 128, :],
                      reads=[], writes=AWQK.wk([8 + ft]))
            AWV = WAbB.role(1024)
            for ft in range(FT):
                p.dma("gpsimd", AWV.blk(ft), dram["awv"][ft * 128:(ft + 1) * 128, :],
                      reads=[], writes=AWV.wk([ft]))
            abv_r = load_row("abv_row", 0)

            QT3 = G2.role(T)
            KT3 = G4.role(T)
            for dst, colsk, boff, qscale in ((QT3, "abq_col", 0, 0.125), (KT3, "abk_col", 8, None)):
                for t in range(NH // 2):
                    for qc in range(QC):
                        bk = (t * QC + qc) % 3
                        lhs = [AWQK.blk(boff + ft, t * 128, (t + 1) * 128) for ft in range(FT)]
                        rhs = [[X2T.blk(ft, qc * CW, qc * CW + CW)] for ft in range(FT)]
                        p.op("tensor", AWQK.rk(range(boff, boff + FT)) + X2T.rk(range(FT)),
                             [psk[bk]],
                             mm_chain([PS[bk][:, 0:CW]], lhs, rhs, None))
                        p.op("vector", [psk[bk], colsk], dst.wk([t]),
                             lambda h, dst=dst, t=t, qc=qc, bk=bk, colsk=colsk, qs=qscale:
                             h.tensor_scalar(dst.blk(t, qc * CW, qc * CW + CW),
                                             PS[bk][:, 0:CW],
                                             COLS[colsk][:, t:t + 1], qs,
                                             op0=ALU.add,
                                             **({"op1": ALU.mult} if qs else {})))
            V65 = GV.role(NH * 65)
            p.op("vector", [], V65.wk(range(TT)),
                 lambda h: h.memset(V65.t[:, 0:TT * NH * 65], 1.0))
            for tt in range(TT):
                bks = [(tt % 2) * 2, (tt % 2) * 2 + 1]
                lhs = [X2T.blk(ft, tt * 128, tt * 128 + 128) for ft in range(FT)]
                rhs = [[AWV.blk(ft, 0, 512), AWV.blk(ft, 512, 1024)] for ft in range(FT)]
                extra = [(consts["ones_b"][0:1, 0:128],
                          [abv_r[0:1, 0:512], abv_r[0:1, 512:1024]])]
                p.op("tensor", X2T.rk(range(FT)) + AWV.rk(range(FT)) + ["ones_b", "brow:0"],
                     [psk[bks[0]], psk[bks[1]]],
                     mm_chain([PS[bks[0]][:, :], PS[bks[1]][:, :]], lhs, rhs, None, extra))
                for c in range(2):
                    def vev(h, tt=tt, c=c, b=bks[c]):
                        dv = V65.blk(tt).rearrange("p (g c) -> p g c", c=65)[:, 8 * c:8 * c + 8, 0:64]
                        return h.tensor_copy(dv, PS[b][:, :].rearrange("p (g c) -> p g c", c=64))
                    p.op("vector", [psk[bks[c]]], V65.wk([tt]), vev)

            CTb = G1.role(T)
            RD = BR
            for h_ in range(NH):
                t, base = h_ // 2, (h_ % 2) * 64
                for qc in range(QC):
                    ktmax = min(TT, (qc + 1) * CW // 128)
                    ctb = 4 if (qc % 2 == 0) else 5

                    def s_mm(kt):
                        bk = kt % 3
                        p.op("tensor", [KT3.k(t), QT3.k(t)], [psk[bk]],
                             lambda h, t=t, base=base, kt=kt, qc=qc, bk=bk:
                             h.matmul(PS[bk][:, 0:CW],
                                      KT3.blk(t, kt * 128, kt * 128 + 128, p0=base, p1=base + 64),
                                      QT3.blk(t, qc * CW, qc * CW + CW, p0=base, p1=base + 64),
                                      start=True, stop=True))
                        b0 = kt * 128 - qc * CW
                        sl = kt % 2
                        p.op("scalar", [psk[bk]], ROT.wk([sl]),
                             lambda h, sl=sl, bk=bk: h.activation(
                                 ROT.blk(sl, 0, CW), PS[bk][:, 0:CW], AF.Exp))
                        if b0 > 0:
                            p.op("vector", [ROT.k(sl)], ROT.wk([sl]),
                                 lambda h, sl=sl, b0=b0: h.memset(ROT.blk(sl, 0, b0), 0.0))
                        if b0 >= 0:
                            p.op("vector", [ROT.k(sl), "trimask"], ROT.wk([sl]),
                                 lambda h, sl=sl, b0=b0: h.tensor_tensor(
                                     ROT.blk(sl, b0, b0 + 128), ROT.blk(sl, b0, b0 + 128),
                                     consts["trimask"][:, :], op=ALU.mult))

                    def pv_mm(kt):
                        sl = kt % 2
                        p.op("tensor", [ROT.k(sl), V65.k(kt)], [psk[ctb]],
                             lambda h, h_=h_, kt=kt, sl=sl, ctb=ctb, ktmax=ktmax: h.matmul(
                                 PS[ctb][0:65, 0:CW],
                                 V65.blk(kt, h_ * 65, (h_ + 1) * 65),
                                 ROT.blk(sl, 0, CW),
                                 start=(kt == 0), stop=(kt == ktmax - 1)))
                    s_mm(0)
                    for kt in range(1, ktmax):
                        s_mm(kt)
                        pv_mm(kt - 1)
                    pv_mm(ktmax - 1)
                    p.op("vector", [psk[ctb], "xst:0"], ["xst:0"],
                         lambda h, ctb=ctb: h.reciprocal(xstage[0:1, 0:CW], PS[ctb][64:65, 0:CW]))
                    p.op("vector", ["xst:0"], ["brow:0"],
                         lambda h: h.tensor_copy(BR[0:1, 0:CW], xstage[0:1, 0:CW]))
                    p.op("tensor", ["brow:0", "ones_b"], [psk[3]],
                         lambda h: h.matmul(PS[3][0:64, 0:CW], consts["ones_b"][0:1, 0:64],
                                            BR[0:1, 0:CW], start=True, stop=True))
                    p.op("scalar", [psk[ctb]], ROT.wk([0]),
                         lambda h, ctb=ctb: h.activation(ROT.blk(0, 0, CW, p0=0, p1=64),
                                                         PS[ctb][0:64, 0:CW], AF.Copy))
                    p.op("vector", [ROT.k(0), psk[3]], CTb.wk([t]),
                         lambda h, t=t, base=base, qc=qc: h.tensor_tensor(
                             CTb.blk(t, qc * CW, qc * CW + CW, p0=base, p1=base + 64),
                             ROT.blk(0, 0, CW, p0=0, p1=64), PS[3][0:64, 0:CW],
                             op=ALU.mult))

            OWT = WAbB.role(1024)
            load_weight(OWT, "out_W", FT)
            outb_r = load_row("outb_row", 0)
            X3 = G4.role(D)
            for tt in range(TT):
                bks = [(tt % 2) * 2, (tt % 2) * 2 + 1]
                lhs = [CTb.blk(ftb, tt * 128, tt * 128 + 128) for ftb in range(FT)]
                rhs = [[OWT.blk(ftb, 0, 512), OWT.blk(ftb, 512, 1024)] for ftb in range(FT)]
                extra = [(consts["ones_b"][0:1, 0:128],
                          [outb_r[0:1, 0:512], outb_r[0:1, 512:1024]])]
                p.op("tensor", CTb.rk(range(FT)) + OWT.rk(range(FT)) + ["ones_b", "brow:0"],
                     [psk[bks[0]], psk[bks[1]]],
                     mm_chain([PS[bks[0]][:, :], PS[bks[1]][:, :]], lhs, rhs, None, extra))
                sl = tt % 2
                p.dma("sync", ROT.blk(sl), dram["x2scr"][tt * 128:(tt + 1) * 128, :],
                      reads=[f"x2scr:{tt}"], writes=ROT.wk([sl]))
                for c in range(2):
                    p.op("vector", [psk[bks[c]], ROT.k(sl)], ROT.wk([sl]),
                         lambda h, sl=sl, c=c, b=bks[c]: h.tensor_tensor(
                             ROT.blk(sl, c * 512, (c + 1) * 512), PS[b][:, :],
                             ROT.blk(sl, c * 512, (c + 1) * 512), op=ALU.add))
                ln_plain_tile(ROT, sl, X3, tt)
            return X3

        def phase4(X3):
            X3T = G1.role(T)
            for tt in range(TT):
                transpose_tile_to_T(X3, tt, X3T, tt)
            load_cols("mu_r_col")
            load_cols("mu_k_col")
            OMU = sb("OMU", [128, 2 * FT], F32)
            for i, v in enumerate(("mu_r_col", "mu_k_col")):
                p.op("vector", [v], [f"omu:{i}"],
                     lambda h, i=i, v=v: h.tensor_scalar(
                         OMU[:, i * FT:(i + 1) * FT], COLS[v][:, :], -1.0, 1.0,
                         op0=ALU.mult, op1=ALU.add))
            WTT = G3.role(1024)
            WVT4 = GV.role(1024)
            for wi, (wname, blks) in enumerate((("W_r", range(0, 8)), ("W_k", range(8, 16)))):
                WS = WAbB.role(1024)
                load_weight(WS, wname, FT)
                for rb in range(FT):
                    transpose_tile_to_T(WS, rb, WTT, rb) if False else None
                for rb in range(FT):
                    done = 0
                    bkk = 0
                    while done < FT:
                        nb = min(4, FT - done)
                        bkk = 1 - bkk

                        def mk(h, rb=rb, done=done, nb=nb, bkk=bkk, WS=WS):
                            last = None
                            for j in range(nb):
                                ft = done + j
                                last = h.transpose(PSB[bkk][:, j * 128:(j + 1) * 128],
                                                   WS.blk(rb, ft * 128, (ft + 1) * 128),
                                                   consts["identb"][:, :])
                            return last
                        p.op("tensor", [WS.k(rb), "identb"], [psbk[bkk]], mk)
                        blist = list(blks)[done:done + nb]

                        def ev(h, rb=rb, nb=nb, bkk=bkk, blist=tuple(blist)):
                            return h.tensor_copy(
                                WTT.view3()[:, blist[0]:blist[0] + nb, rb * 128:rb * 128 + 128],
                                PSB[bkk][:, 0:nb * 128].rearrange("p (b c) -> p b c", c=128))
                        p.op("vector", [psbk[bkk]], WTT.wk(blist), ev)
                        done += nb
            WS = WAbB.role(1024)
            load_weight(WS, "W_v", FT)
            for rb in range(FT):
                transpose_tile_to_T(WS, rb, WVT4, rb)

            CH = G4.role(512) if False else None
            X4 = X3
            CHB = GV
            CH = G2.role(CW)

            def mixes(qc):
                for mi, boff in ((0, 0), (1, 8)):
                    mcol = ("mu_r_col", "mu_k_col")[mi]
                    for ft in range(FT):
                        p.op("vector", [X3T.k(ft), mcol], CH.wk([boff + ft]),
                             lambda h, ft=ft, qc=qc, boff=boff, mcol=mcol: h.tensor_scalar(
                                 CH.blk(boff + ft), X3T.blk(ft, qc * CW, qc * CW + CW),
                                 COLS[mcol][:, ft:ft + 1], None, op0=ALU.mult))
                        if qc == 0:
                            p.op("vector", [X3T.k(ft), f"omu:{mi}", CH.k(boff + ft)],
                                 CH.wk([boff + ft]),
                                 lambda h, ft=ft, boff=boff, mi=mi: h.scalar_tensor_tensor(
                                     CH.blk(boff + ft, 1, CW),
                                     X3T.blk(ft, 0, CW - 1),
                                     OMU[:, mi * FT + ft:mi * FT + ft + 1],
                                     CH.blk(boff + ft, 1, CW),
                                     op0=ALU.mult, op1=ALU.add))
                        else:
                            p.op("vector", [X3T.k(ft), f"omu:{mi}", CH.k(boff + ft)],
                                 CH.wk([boff + ft]),
                                 lambda h, ft=ft, qc=qc, boff=boff, mi=mi: h.scalar_tensor_tensor(
                                     CH.blk(boff + ft, 0, CW),
                                     X3T.blk(ft, qc * CW - 1, (qc + 1) * CW - 1),
                                     OMU[:, mi * FT + ft:mi * FT + ft + 1],
                                     CH.blk(boff + ft, 0, CW),
                                     op0=ALU.mult, op1=ALU.add))

            def rk_kv(qc):
                for oi, (wblks, srcoff, dstoff, actf) in enumerate((
                        (range(0, 8), 0, 16, AF.Sigmoid), (range(8, 16), 8, 24, AF.Relu))):
                    for dt in range(FT):
                        bk = dt % 3
                        lhs = [WTT.blk(list(wblks)[ft], dt * 128, dt * 128 + 128)
                               for ft in range(FT)]
                        rhs = [[CH.blk(srcoff + ft)] for ft in range(FT)]
                        p.op("tensor", WTT.rk(wblks) + CH.rk(range(srcoff, srcoff + FT)),
                             [psk[bk]], mm_chain([PS[bk][:, 0:CW]], lhs, rhs, None))
                        p.op("scalar", [psk[bk]], CH.wk([dstoff + dt]),
                             lambda h, dt=dt, bk=bk, dstoff=dstoff, actf=actf: h.activation(
                                 CH.blk(dstoff + dt), PS[bk][:, 0:CW], actf))
                for dt in range(FT):
                    p.op("vector", [CH.k(24 + dt)], CH.wk([24 + dt]),
                         lambda h, dt=dt: h.tensor_tensor(CH.blk(24 + dt), CH.blk(24 + dt),
                                                          CH.blk(24 + dt), op=ALU.mult))
                for dt in range(FT):
                    bk = dt % 3
                    lhs = [WVT4.blk(ft, dt * 128, dt * 128 + 128) for ft in range(FT)]
                    rhs = [[CH.blk(24 + ft)] for ft in range(FT)]
                    p.op("tensor", WVT4.rk(range(FT)) + CH.rk(range(24, 32)),
                         [psk[bk]], mm_chain([PS[bk][:, 0:CW]], lhs, rhs, None))
                    p.op("vector", [psk[bk], CH.k(16 + dt)], CH.wk([16 + dt]),
                         lambda h, dt=dt, bk=bk: h.tensor_tensor(
                             CH.blk(16 + dt), PS[bk][:, 0:CW], CH.blk(16 + dt), op=ALU.mult))

            def tail(qc):
                ntt = CW // 128
                for j in range(ntt):
                    tt = qc * ntt + j
                    sl = 1
                    done = 0
                    bkk = 0
                    while done < FT:
                        nb = min(4, FT - done)
                        bkk = 1 - bkk

                        def mk(h, j=j, done=done, nb=nb, bkk=bkk, CH=CH):
                            last = None
                            for jj in range(nb):
                                last = h.transpose(PSB[bkk][:, jj * 128:(jj + 1) * 128],
                                                   CH.blk(16 + done + jj, j * 128, (j + 1) * 128),
                                                   consts["identb"][:, :])
                            return last
                        p.op("tensor", CH.rk(range(16 + done, 16 + done + nb)) + ["identb"],
                             [psbk[bkk]], mk)

                        def ev(h, done=done, nb=nb, bkk=bkk, sl=sl):
                            return h.tensor_copy(
                                ROT.blk(sl, done * 128, done * 128 + nb * 128).rearrange(
                                    "p (b c) -> p b c", c=128),
                                PSB[bkk][:, 0:nb * 128].rearrange("p (b c) -> p b c", c=128))
                        p.op("vector", [psbk[bkk]], ROT.wk([sl]), ev)
                        done += nb
                    p.op("vector", [ROT.k(sl), X4.k(tt)], ROT.wk([sl]),
                         lambda h, tt=tt, sl=sl: h.tensor_tensor(
                             ROT.blk(sl), ROT.blk(sl), X4.blk(tt), op=ALU.add))
                    ln_plain_tile(ROT, sl, X4, tt)

            mixes(0)
            for qc in range(QC):
                rk_kv(qc)
                if qc + 1 < QC:
                    mixes(qc + 1)
                tail(qc)
            return X4

        wq_thunks = []
        wq_bct = [False]

        def wq_local_thunks():
            NWT = FI // 128

            def tile_thunk(srcdram, r0w, dstdram, d0w, cc, cw):
                def th():
                    sl = (d0w + cc) // 512 % 2
                    if wq_bct[0]:
                        dstap, dstk = BCT[sl][:, 0:cw], f"bct:{sl}"
                    else:
                        dstap, dstk = ROT.blk(sl, 0, cw), ROT.k(sl)
                    p.dma("gpsimd", xstage[:, 0:cw], srcdram[r0w:r0w + 128, cc:cc + cw],
                          reads=["xst:0"], writes=["xst:0"])
                    quant_qd(xstage[:, 0:cw], "xst:0", dstap, dstk, cw)
                    p.dma("gpsimd", dstdram[d0w:d0w + 128, cc:cc + cw], dstap,
                          reads=[dstk], writes=[f"wgo:{dstdram is dram['wgoB']}:{d0w}:{cc}"])
                return th
            NWTD = FI // 128
            for t in range(NWTD):
                wq_thunks.append(tile_thunk(dram["w_gate"], t * 128, dram["wgoA"], t * 128, 0, D))
            for t in range(NWTD):
                wq_thunks.append(tile_thunk(dram["w_up"], t * 128, dram["wgoA"],
                                            (NWTD + t) * 128, 0, D))
            for t in range(FT):
                for cc, cw in ((0, 1024), (1024, 1024), (2048, 768)):
                    wq_thunks.append(tile_thunk(dram["w_down"], t * 128, dram["wgoB"],
                                                t * 128, cc, cw))

        def bg_drain(n=10**9):
            while wq_thunks and n > 0:
                wq_thunks.pop(0)()
                n -= 1

        if upto >= 2 and shard_wq == "local":
            wq_local_thunks()
        XT = phase0()
        stream = phase1(XT)
        bg_drain()
        if upto >= 2:
            stream = phase2(stream, G1, G3, G4, GV)
        if upto >= 3:
            stream = phase3(stream)
        if upto >= 4:
            stream = phase4(stream)

        for tt in range(TT):
            p.op("scalar", [stream.k(tt), "xst:0"], ["xst:0"],
                 lambda h, tt=tt: h.copy(xstage[:, 0:D], stream.blk(tt)))
            p.dma("sync", dram["y"][tt * 128:(tt + 1) * 128, :],
                  xstage[:, 0:D],
                  reads=["xst:0"], writes=["xst:0"])
        p.wait_all("sync")

        block = ctx.enter_context(nc.Block())
        p.flush(block)

    return nc, p


def prep_consts(T):
    TT = T // 128
    headmask = np.zeros((128, FT * NH), np.float32)
    for ft in range(FT):
        for pp in range(128):
            h = (ft * 128 + pp) // HD
            headmask[pp, ft * NH + h] = 1.0
    tri = np.tril(np.ones((128, 128), np.float32)).astype(BF)
    return {
        "identb": np.eye(128).astype(BF),
        "identf": np.eye(128, dtype=np.float32),
        "ones_b": np.ones((1, 128), BF),
        "headmask": headmask,
        "trimask": np.triu(np.ones((128, 128), np.float32)).astype(BF),
    }


def prep_phase1_weights(ins):
    d = {}
    for w in ("Wq", "Wk", "Wv", "Wfq", "Wfk"):
        d[w] = np.ascontiguousarray(ins["L0_attn_" + w])
    d["W1"] = np.ascontiguousarray(ins["L0_attn_W1"])
    d["W2"] = np.ascontiguousarray(ins["L0_attn_W2"])
    col = lambda v: np.ascontiguousarray(v.reshape(-1, 128).T)
    d["bfq_col"] = col(ins["L0_attn_bfq"])
    d["b1_col"] = col(ins["L0_attn_b1"])
    d["gq_col"] = col(ins["L0_attn_gq"])
    d["betaq_col"] = col(ins["L0_attn_betaq"])
    d["gk_col"] = col(ins["L0_attn_gk"])
    d["betak_col"] = col(ins["L0_attn_betak"])
    for v, src in (("gv", "gv"), ("betav", "betav"), ("ga", "ga"), ("ba", "ba")):
        d[v + "_bc"] = np.ascontiguousarray(
            np.broadcast_to(ins["L0_attn_" + src][None, :], (128, D)))
    for v, src in (("bq_row", "bq"), ("bk_row", "bk"), ("bv_row", "bv"),
                   ("bfk_row", "bfk"), ("b2_row", "b2")):
        d[v] = np.ascontiguousarray(ins["L0_attn_" + src][None, :])
    return d


def prep_phase2_weights(ins):
    return {
        "w_gate": np.ascontiguousarray(ins["L0_ffn_w_gate"]),
        "w_up": np.ascontiguousarray(ins["L0_ffn_w_up"]),
        "w_down": np.ascontiguousarray(ins["L0_ffn_w_down"]),
    }


def prep_phase3_weights(ins):
    ab = np.asarray(ins["L1_attn_attn_b"]).reshape(NH, 3 * HD)
    abq = np.zeros((128, NH), np.float32)
    abk = np.zeros((128, NH), np.float32)
    for t in range(NH // 2):
        abq[0:64, t] = ab[2 * t, 0:64];   abq[64:128, t] = ab[2 * t + 1, 0:64]
        abk[0:64, t] = ab[2 * t, 64:128]; abk[64:128, t] = ab[2 * t + 1, 64:128]
    abv = ab[:, 128:192].reshape(1, D)
    W = np.asarray(ins["L1_attn_attn_W"]).reshape(D, NH, 3 * HD)
    awq = W[:, :, 0:64].reshape(D, D)
    awk = W[:, :, 64:128].reshape(D, D)
    awv = W[:, :, 128:192].reshape(D, D)
    return {
        "awq": np.ascontiguousarray(awq),
        "awk": np.ascontiguousarray(awk),
        "awv": np.ascontiguousarray(awv),
        "out_W": np.ascontiguousarray(ins["L1_attn_out_W"]),
        "abq_col": abq, "abk_col": abk,
        "abv_row": np.ascontiguousarray(abv),
        "outb_row": np.ascontiguousarray(ins["L1_attn_out_b"][None, :]),
    }


def prep_phase4_weights(ins):
    col = lambda v: np.ascontiguousarray(v.reshape(-1, 128).T)
    return {
        "W_r": np.ascontiguousarray(ins["L1_ffn_W_r"]),
        "W_k": np.ascontiguousarray(ins["L1_ffn_W_k"]),
        "W_v": np.ascontiguousarray(ins["L1_ffn_W_v"]),
        "mu_r_col": col(ins["L1_ffn_mu_r"]),
        "mu_k_col": col(ins["L1_ffn_mu_k"]),
    }


def kernel(**inputs):
    T = 2048
    UPTO = 4
    nc, _ = build_core_program(T=T, upto=UPTO, shard_wq="local")
    consts = prep_consts(T)
    wts = prep_phase1_weights(inputs)
    if UPTO >= 2:
        wts.update(prep_phase2_weights(inputs))
    if UPTO >= 3:
        wts.update(prep_phase3_weights(inputs))
    if UPTO >= 4:
        wts.update(prep_phase4_weights(inputs))
    x = np.asarray(inputs["L0_attn_x"])
    in_maps = []
    for c in range(8):
        m = dict(consts)
        m.update(wts)
        m["x"] = np.ascontiguousarray(x[c])
        in_maps.append(m)
    res = run_bass_kernel_spmd(nc, in_maps, core_ids=list(range(8)))
    return np.stack([res.results[c]["y"] for c in range(8)], axis=0)
